# version 20
# baseline (speedup 1.0000x reference)
import numpy as np
from collections import defaultdict
from contextlib import ExitStack

import concourse.bass as bass
import concourse.mybir as mybir
from concourse.bass_utils import run_bass_kernel_spmd

F32 = mybir.dt.float32
BF16 = mybir.dt.bfloat16
U8 = mybir.dt.uint8
AF = mybir.ActivationFunctionType
ALU = mybir.AluOpType
AX = mybir.AxisListType
bcast = bass.broadcast_tensor_aps

NTOK = 8192
D = 1024
NCH = 16
CH = 512
NT = 4
NFD = 8
F1T = 32
F2T = 32
NTILES = NCH * NT
FPC = 8 + F2T


class B:
    def __init__(self, nc, sems, t, phases, debug):
        self.nc, self.sems, self.t = nc, sems, t
        self.phases, self.debug = phases, debug

    def run(self, eng, handle):
        self.eng, self.h = eng, handle
        self.cnt = defaultdict(int)
        self.emit()

    def on(self, eng):
        return self.h if self.eng == eng else None

    def w(self, eng, sem, val):
        if self.eng == eng and val > 0:
            self.h.wait_ge(self.sems[sem], val)

    def inc(self, sem, n=1):
        self.cnt[sem] += n
        return self.cnt[sem]

    def fin(self, inst, sem, n=1):
        if inst is not None:
            inst.then_inc(self.sems[sem], n)
        return self.inc(sem, n)

    def vbar(self, inst):
        val = self.fin(inst, 'vbr')
        self.w('vector', 'vbr', val)

    def emit(self):
        self.init()
        self.prep()
        self.a1()
        if self.phases >= 2:
            self.a2()
        if self.phases >= 3:
            self.a3()
        if self.phases >= 4:
            self.f()
        self.tail()

    def trans_1024(self, src, dst, tag_t, tag_e):
        t = self.t
        te, a = self.on('tensor'), self.on('scalar')
        for arow in range(8):
            bank = t.ptr[arow % 2]
            self.w('tensor', tag_e, self.cnt[tag_e] - 1)
            last = None
            if te:
                for jb in range(8):
                    last = te.transpose(bank[:, jb * 128:(jb + 1) * 128],
                                        src[:, arow * 1024 + jb * 128: arow * 1024 + jb * 128 + 128],
                                        t.ident[:])
            trd = self.fin(last, tag_t)
            self.w('scalar', tag_t, trd)
            ei = None
            if a:
                outap = dst[:].rearrange("p (j i) -> p j i", j=8)[:, :, arow * 128:arow * 128 + 128]
                ei = a.copy(outap, bank[:].rearrange("p (j i) -> p j i", j=8))
            self.fin(ei, tag_e)

    def init(self):
        t = self.t
        g, s, v = self.on('gpsimd'), self.on('sync'), self.on('vector')
        te, a = self.on('tensor'), self.on('scalar')

        last = None
        if g:
            g.memset(t.zv[:], 0.0)
            g.memset(t.ov[:], 1.0)
            g.memset(t.epsv[:], 1e-5)
            g.memset(t.onesrow[:], 1.0)
            g.memset(t.ident[:], 0.0)
            g.affine_select(out=t.ident[:], in_=t.ident[:], compare_op=ALU.not_equal,
                            fill=1.0, base=0, pattern=[[-1, 128]], channel_multiplier=1)
            g.memset(t.E[:], 0.0)
            for f in range(NFD):
                g.memset(t.E[0:64, f * 16 + 2 * f: f * 16 + 2 * f + 1], 1.0)
                last = g.memset(t.E[64:128, f * 16 + 2 * f + 1: f * 16 + 2 * f + 2], 1.0)
        ini = self.fin(last, 'ini')

        if s:
            s.dma_start(t.wq32[:], t.wqkv.rearrange("(a p) f -> p a f", p=128)).then_inc(self.sems['ild'], 16)
            s.dma_start(t.wo32[:], t.wout.rearrange("(a p) f -> p a f", p=128)).then_inc(self.sems['ild'], 16)
            s.dma_start(t.b32[:], t.bout).then_inc(self.sems['ild'], 16)
            for j, st in enumerate([t.s0, t.s1, t.s2]):
                s.dma_start(t.svec[:, j:j + 1], st).then_inc(self.sems['ild'], 16)
            s.dma_start(t.tempB[:], bass.AP(t.temp.tensor, 0, [[0, 128], [1, 16]])).then_inc(self.sems['ild'], 16)
        ild = self.inc('ild', 16 * 7)

        self.w('vector', 'ild', ild)
        last = None
        if v:
            v.memset(t.der[:], 0.0)
            v.tensor_copy(t.wqb[:], t.wq32[:])
            v.tensor_copy(t.wob[:], t.wo32[:])
            v.tensor_copy(t.b_bf[:], t.b32[:])
            v.tensor_tensor(t.der[:, 0:1], t.svec[:, 0:1], t.svec[:, 0:1], ALU.add)
            v.tensor_tensor(t.der[:, 1:2], t.svec[:, 1:2], t.svec[:, 1:2], ALU.add)
            last = v.tensor_tensor(t.der[:, 2:3], t.svec[:, 2:3], t.svec[:, 2:3], ALU.add)
        inv = self.fin(last, 'inv')

        self.w('gpsimd', 'inv', inv)
        if g:
            g.dma_start(t.der_h, t.der[:]).then_inc(self.sems['dsc'], 16)
        d1 = self.inc('dsc', 16)
        self.w('gpsimd', 'dsc', d1)
        if g:
            g.dma_start(t.scalsB[:], bass.AP(t.der_h.tensor, 0, [[0, 128], [1, 8]])).then_inc(self.sems['dsc'], 16)
        self.inc('dsc', 16)

        self.w('tensor', 'inv', inv)
        self.w('tensor', 'ini', ini)
        self.trans_1024(t.wob, t.wqT, 'wtt', 'wts')
        wtsv = self.cnt['wts']
        self.w('scalar', 'wts', wtsv)
        ei = None
        if a:
            ei = a.dma_start(t.woT_h, t.wqT[:])
            ei.then_inc(self.sems['wth'], 16)
        wth = self.inc('wth', 16)
        self.w('tensor', 'wth', wth)
        self.trans_1024(t.wqb, t.wqT, 'wtt', 'wts')
        wtb = self.fin(te.nop() if te else None, 'wtb')

        for f in range(NFD):
            self.w('tensor', 'm2v', self.cnt['m2v'])
            last = None
            if te:
                last = te.transpose(t.pitp[0][:16, 0:128], t.E[:, f * 16:(f + 1) * 16], t.ident[:])
            m2t = self.fin(last, 'm2t')
            self.w('vector', 'm2t', m2t)
            mi = None
            if v:
                mi = v.tensor_copy(t.M2[:16, f * 128:(f + 1) * 128], t.pitp[0][:16, 0:128])
            self.fin(mi, 'm2v')

    def prep(self):
        t = self.t
        g = self.on('gpsimd')
        self.w('gpsimd', 'dsc', self.cnt['dsc'])

        def pldsem(kt):
            return 'pldA' if kt % 2 == 0 else 'pldB'

        def k2load(kt):
            if g:
                g.dma_start(t.pA2[kt % 2][:, 0:1024],
                            t.k2.rearrange("(a p) f -> p a f", p=128)[:, kt, :]) \
                    .then_inc(self.sems[pldsem(kt)], 16)
            self.inc(pldsem(kt), 16)

        k2load(0)
        k2load(1)
        for kc in range(F2T):
            pc = kc % 2
            wsem = 'pw2A' if pc == 0 else 'pw2B'
            self.w('gpsimd', pldsem(kc), self.cnt[pldsem(kc)])
            self.w('gpsimd', wsem, self.cnt[wsem])
            if g:
                g.tensor_scalar(t.pD2[pc][:], t.pA2[pc][:, 0:1024], 2.0, -1.0,
                                ALU.mult, ALU.add)
                g.dma_start(t.w2s_h[:, kc * 1024:(kc + 1) * 1024], t.pD2[pc][:]) \
                    .then_inc(self.sems[wsem], 16)
            self.inc(wsem, 16)
            if kc + 2 < F2T:
                k2load(kc + 2)


    def a1(self):
        t = self.t
        self.w('scalar', 'ini', 1)
        for c in range(NCH):
            cp = c % 2
            s, v, a, te = self.on('sync'), self.on('vector'), self.on('scalar'), self.on('tensor')
            sl = slice(cp * NT, cp * NT + NT)
            for tt in range(NT):
                gidx = c * NT + tt
                self.w('sync', 'lnf', (c - 1) * NT + tt + 1)
                xsem = f'xld{tt}'
                if s:
                    s.dma_start(t.xbuf4[tt][:], t.x[gidx * 128:(gidx + 1) * 128, :]) \
                        .then_inc(self.sems[xsem], 16)
                xv = self.inc(xsem, 16)
                self.w('scalar', xsem, xv)
                ri = None
                if a:
                    ri = a.activation(t.junk[:], t.xbuf4[tt][:], AF.Copy,
                                      accum_out=t.lnsum[:, cp * NT + tt:cp * NT + tt + 1])
                self.fin(ri, 'lsg')
                self.w('scalar', 'lnA', c - 1)
                sqi = None
                if a:
                    sqi = a.activation(t.junk[:], t.xbuf4[tt][:], AF.Square,
                                       accum_out=t.ssqs[:, cp * NT + tt:cp * NT + tt + 1])
                self.fin(sqi, 'sqd')
            self.w('vector', 'lsg', self.cnt['lsg'])
            self.w('vector', 'sqd', self.cnt['sqd'])
            ri = None
            if v:
                v.tensor_scalar_mul(t.lnm[:, sl], t.lnsum[:, sl], 1.0 / D)
                ri = v.tensor_scalar_mul(t.lnt1[:, sl], t.ssqs[:, sl], 1.0 / D)
            self.vbar(ri)
            if v:
                ri = v.tensor_tensor(t.lnmsq[:, sl], t.lnm[:, sl], t.lnm[:, sl], ALU.mult)
            self.vbar(ri)
            lnA = None
            if v:
                lnA = v.tensor_tensor(t.lnvar[:, sl], t.lnt1[:, sl], t.lnmsq[:, sl], ALU.subtract)
            va = self.fin(lnA, 'lnA')
            self.w('scalar', 'lnA', va)
            sdi = None
            if a:
                sdi = a.activation(t.lnsd[:, sl], t.lnvar[:, sl], AF.Sqrt, bias=t.epsv[:])
            vb = self.fin(sdi, 'lnB')
            self.w('vector', 'lnB', vb)
            ri = None
            if v:
                ri = v.reciprocal(t.lnrstd[:, sl], t.lnsd[:, sl])
            rr = self.fin(ri, 'lnr')
            self.w('vector', 'lnr', rr)
            for tt in range(NT):
                gidx = c * NT + tt
                par = gidx % 2
                self.w('vector', 'trp', gidx - 1)
                xni = None
                if v:
                    xni = v.tensor_scalar(t.xnbuf[:, par * D:(par + 1) * D], t.xbuf4[tt][:],
                                          t.lnm[:, cp * NT + tt:cp * NT + tt + 1],
                                          t.lnrstd[:, cp * NT + tt:cp * NT + tt + 1],
                                          ALU.subtract, ALU.mult)
                lnf = self.fin(xni, 'lnf')

                self.w('tensor', 'lnf', lnf)
                self.w('tensor', 'xev', gidx - 1)
                if gidx == 0:
                    self.w('tensor', 'wts', 16)
                last = None
                if te:
                    for f in range(NFD):
                        last = te.transpose(t.ptr[par][:, f * 128:(f + 1) * 128],
                                            t.xnbuf[:, par * D + f * 128: par * D + f * 128 + 128],
                                            t.ident[:])
                trp = self.fin(last, 'trp')

                h = tt // 2
                hidx = 2 * c + h
                slot = hidx % 2
                self.w('scalar', 'trp', trp)
                if tt % 2 == 0 and hidx >= 2:
                    self.w('scalar', 'qkm', NFD * (hidx - 1))
                    self.w('scalar', 'xns', 16 * (hidx - 1))
                ev = None
                if a:
                    outap = t.xnTcH[slot][:].rearrange("p (f tok) -> p f tok", f=NFD)[:, :, (tt % 2) * 128:(tt % 2) * 128 + 128]
                    ev = a.copy(outap, t.ptr[par][:].rearrange("p (f i) -> p f i", f=NFD))
                self.fin(ev, 'xev')
                if tt % 2 == 0:
                    continue
                self.w('scalar', 'xev', self.cnt['xev'])
                if a:
                    dst = t.xnt_h[:, :, c * CH + h * 256: c * CH + (h + 1) * 256].rearrange("f p tok -> p f tok")
                    a.dma_start(dst, t.xnTcH[slot][:].rearrange("p (f tok) -> p f tok", f=NFD)) \
                        .then_inc(self.sems['xns'], 16)
                self.inc('xns', 16)
                self.w('tensor', 'xev', self.cnt['xev'])
                if hidx == 0:
                    self.w('tensor', 'wtb', 1)
                    self.w('tensor', 'm2v', NFD)
                for i in range(NFD):
                    ghidx = hidx * NFD + i
                    bank = t.qkvp[ghidx % 6]
                    self.w('tensor', 'wve', ghidx - 5)
                    last = None
                    if te:
                        for j in range(NFD):
                            last = te.matmul(bank[:, 0:256],
                                             t.wqT[:, j * 1024 + i * 128: j * 1024 + i * 128 + 128],
                                             t.xnTcH[slot][:, j * 256:(j + 1) * 256],
                                             start=(j == 0), stop=(j == NFD - 1))
                    qk = self.fin(last, 'qkm')
                    self.w('scalar', 'qkm', qk)
                    wv = None
                    if a:
                        wv = a.copy(t.wT[:, i * NTOK + c * CH + h * 256: i * NTOK + c * CH + h * 256 + 256],
                                    bank[:, 0:256])
                    self.fin(wv, 'wve')
                    if h == 1:
                        nsq = None
                        if a:
                            nsq = a.activation(t.junkb[:], t.wT[:, i * NTOK + c * CH: i * NTOK + (c + 1) * CH],
                                               AF.Square, accum_out=t.nsqacc[:, i * NCH + c: i * NCH + c + 1])
                        self.fin(nsq, 'nsq')

        v = self.on('vector')
        self.w('vector', 'nsq', NFD * NCH)
        ri = None
        if v:
            ri = v.tensor_reduce(t.rn2a[:], t.nsqacc[:].rearrange("p (f c) -> p f c", f=NFD), AX.X, ALU.add)
        self.vbar(ri)
        if v:
            ri = v.tensor_scalar_max(t.rn2b[:], t.rn2a[:], 1e-24)
        self.vbar(ri)
        if v:
            ri = v.reciprocal(t.rn2[:], t.rn2b[:])
        rn = self.fin(ri, 'rnd')
        self.w('vector', 'rnd', rn)
        ri = None
        if v:
            for f in range(NFD):
                ri = v.tensor_scalar_mul(t.E[:, f * 16:(f + 1) * 16], t.E[:, f * 16:(f + 1) * 16],
                                         t.rn2[:, f:f + 1])
        self.vbar(ri)
        ni = self.on('vector').nop() if self.on('vector') else None
        self.fin(ni, 'epv')

    def a2_front1a(self, c):
        t = self.t
        cp = c % 2
        v, te = self.on('vector'), self.on('tensor')
        a = self.on('scalar')
        self.w('scalar', 'emm', c - 1)
        self.w('scalar', 'ust', NFD * (c - 1))
        if c == 0:
            self.w('scalar', 'qkm', 2 * NFD * NCH)
            self.w('scalar', 'xns', 32 * NCH)
        wni = None
        for f in range(NFD):
            if a:
                wni = a.activation(t.wn2f[cp][:, f * CH:(f + 1) * CH],
                                   t.wT[:, f * NTOK + c * CH: f * NTOK + (c + 1) * CH],
                                   AF.Square)
        wn = self.fin(wni, 'wnv')
        self.w('tensor', 'wnv', wn)
        if c == 0:
            self.w('tensor', 'm2v', NFD)
            self.w('tensor', 'epv', 1)
        self.w('tensor', 'lgv', c - 1)
        last = None
        if te:
            for f in range(NFD):
                last = te.matmul(t.lgp[cp][:16, :], t.E[:, f * 16:(f + 1) * 16],
                                 t.wn2f[cp][:, f * CH:(f + 1) * CH],
                                 start=(f == 0), stop=(f == NFD - 1))
        self.fin(last, 'emm')

    def a2_front1b(self, c):
        t = self.t
        cp = c % 2
        v, te = self.on('vector'), self.on('tensor')
        self.w('vector', 'emm', c + 1)
        self.w('vector', 'lgt', c - 1)
        lv = None
        if v:
            lv = v.tensor_copy(t.lgT[:16, cp * CH:(cp + 1) * CH], t.lgp[cp][:16, :])
        lgv = self.fin(lv, 'lgv')
        self.w('tensor', 'lgv', lgv)
        self.w('tensor', 'pnv', c - 1)
        last = None
        if te:
            for q in range(NT):
                last = te.transpose(t.lgnp[cp][:, q * 16:(q + 1) * 16],
                                    t.lgT[:16, cp * CH + q * 128: cp * CH + (q + 1) * 128],
                                    t.ident[:16, 0:16])
        self.fin(last, 'lgt')

    def a2_front2(self, c):
        t = self.t
        cp = c % 2
        v, te, a = self.on('vector'), self.on('tensor'), self.on('scalar')
        self.w('vector', 'lgt', c + 1)
        lt3 = t.ltbuf[:, cp * 64:(cp + 1) * 64].rearrange("p (q h) -> p q h", q=NT)
        lti = None
        if v:
            ln3 = t.lgnp[cp][:].rearrange("p (q h) -> p q h", q=NT)
            tb3 = t.tempB[:].rearrange("p (o h) -> p o h", o=1)
            ln3b, tb3b = bcast(ln3, tb3)
            lti = v.tensor_tensor(lt3, ln3b, tb3b, ALU.mult)
        self.fin(lti, 'pnv')
        sfi = None
        if v:
            sfi = v.tensor_reduce(t.nmx4[:, cp * 4:(cp + 1) * 4], lt3, AX.X, ALU.max, negate=True)
        self.vbar(sfi)
        if v:
            nm3 = t.nmx4[:, cp * 4:(cp + 1) * 4].rearrange("p (q o) -> p q o", o=1)
            lt3b, nm3b = bcast(lt3, nm3)
            sfi = v.tensor_tensor(t.ebuf[:, cp * 64:(cp + 1) * 64].rearrange("p (q h) -> p q h", q=NT),
                                  lt3b, nm3b, ALU.add)
        sfa = self.fin(sfi, 'sfa')
        self.w('scalar', 'sfa', sfa)
        self.w('scalar', 'pnd', c - 1)
        ei = None
        if a:
            ei = a.activation(t.ebuf2[:, cp * 64:(cp + 1) * 64],
                              t.ebuf[:, cp * 64:(cp + 1) * 64], AF.Exp)
        sfb = self.fin(ei, 'sfb')
        self.w('vector', 'sfb', sfb)
        self.w('vector', 'ptm', c - 1)
        pni = None
        e3 = None
        if v:
            e3 = t.ebuf2[:, cp * 64:(cp + 1) * 64].rearrange("p (q h) -> p q h", q=NT)
            pni = v.tensor_reduce(t.s4[:, cp * 4:(cp + 1) * 4], e3, AX.X, ALU.add)
        self.vbar(pni)
        if v:
            pni = v.reciprocal(t.rs4[:, cp * 4:(cp + 1) * 4], t.s4[:, cp * 4:(cp + 1) * 4])
        self.vbar(pni)
        if v:
            rs3 = t.rs4[:, cp * 4:(cp + 1) * 4].rearrange("p (q o) -> p q o", o=1)
            e3b, rs3b = bcast(e3, rs3)
            pni = v.tensor_tensor(t.pinb[:, cp * 64:(cp + 1) * 64].rearrange("p (q h) -> p q h", q=NT),
                                  e3b, rs3b, ALU.mult)
        self.fin(pni, 'pnd')
        self.w('tensor', 'pnd', c + 1)
        self.w('tensor', 'ptv', c - 1)
        last = None
        if te:
            for q in range(NT):
                last = te.transpose(t.pitp[cp][:16, q * 128:(q + 1) * 128],
                                    t.pinb[:, cp * 64 + q * 16: cp * 64 + (q + 1) * 16],
                                    t.ident[:])
        self.fin(last, 'ptm')

    def a2_back(self, c):
        t = self.t
        cp = c % 2
        v, te, s = self.on('vector'), self.on('tensor'), self.on('sync')
        self.w('vector', 'ptm', c + 1)
        self.w('vector', 'pbm', NFD * (c - 1))
        self.w('vector', 'ptsA' if cp == 0 else 'ptsB', 16 * (c // 2))
        pv = None
        if v:
            v.tensor_copy(t.piTc[:16, cp * CH:(cp + 1) * CH], t.pitp[cp][:16, :])
            pv = v.tensor_reduce(t.Sacc[:16, c:c + 1],
                                 t.piTc[:16, cp * CH:(cp + 1) * CH], AX.X, ALU.add)
        ptv = self.fin(pv, 'ptv')
        self.w('sync', 'ptv', ptv)
        if s:
            s.dma_start(t.pit_h[:, c * CH:(c + 1) * CH],
                        t.piTc[:16, cp * CH:(cp + 1) * CH]) \
                .then_inc(self.sems['ptsA' if cp == 0 else 'ptsB'], 16)
        self.inc('ptsA' if cp == 0 else 'ptsB', 16)

        for f in range(NFD):
            idx = c * NFD + f
            bank = t.pibu[idx % 2]
            self.w('tensor', 'ptv', ptv)
            self.w('tensor', 'ust', idx - 1)
            last = None
            if te:
                last = te.matmul(bank[:], t.M2[:16, f * 128:(f + 1) * 128],
                                 t.piTc[:16, cp * CH:(cp + 1) * CH],
                                 start=True, stop=True)
            pb = self.fin(last, 'pbm')
            self.w('vector', 'pbm', pb)
            ui = None
            if v:
                ui = v.scalar_tensor_tensor(t.ujunk[:], t.wn2f[cp][:, f * CH:(f + 1) * CH],
                                            0.0, bank[:], ALU.bypass, ALU.mult,
                                            accum_out=t.Uacc[:, f * NCH + c: f * NCH + c + 1])
            self.fin(ui, 'ust')

    def a2(self):
        t = self.t
        for c in range(NCH):
            self.a2_front1a(c)
            if c >= 1:
                self.a2_back(c - 1)
            self.a2_front1b(c)
            self.a2_front2(c)
        self.a2_back(NCH - 1)

        v, g = self.on('vector'), self.on('gpsimd')
        self.w('vector', 'ust', NFD * NCH)
        sv = None
        if v:
            sv = v.tensor_reduce(t.Sfin[:16, :], t.Sacc[:16, :], AX.X, ALU.add)
        ssv = self.fin(sv, 'ssv')
        self.w('vector', 'ssv', ssv)
        ni = None
        if v:
            ni = v.nop()
        ssw = self.fin(ni, 'ssw')
        self.w('gpsimd', 'ssw', ssw)
        if g:
            g.dma_start(t.s_h, t.Sfin[:16, :]).then_inc(self.sems['dsc'], 16)
        d3 = self.inc('dsc', 16)
        self.w('gpsimd', 'dsc', d3)
        if g:
            g.dma_start(t.SBfull[:], bass.AP(t.s_h.tensor, 0, [[0, 128], [1, 16]])) \
                .then_inc(self.sems['dsc'], 16)
        d4 = self.inc('dsc', 16)
        self.w('vector', 'dsc', d4)
        ci = None
        if v:
            for f in range(NFD):
                v.tensor_copy(t.SB[0:64, f:f + 1], t.SBfull[0:64, 2 * f:2 * f + 1])
                ci = v.tensor_copy(t.SB[64:128, f:f + 1], t.SBfull[64:128, 2 * f + 1:2 * f + 2])
        self.vbar(ci)
        if v:
            ci = v.tensor_reduce(t.Ufin[:], t.Uacc[:].rearrange("p (f c) -> p f c", f=NFD), AX.X, ALU.add)
        self.vbar(ci)
        if v:
            ci = v.tensor_scalar_add(t.ct1[:], t.SB[:], 1e-8)
        self.vbar(ci)
        if v:
            ci = v.reciprocal(t.ct2[:], t.ct1[:])
        self.vbar(ci)
        if v:
            ci = v.tensor_tensor(t.ct3[:], t.Ufin[:], t.ct2[:], ALU.mult)
        self.vbar(ci)
        if v:
            ci = v.tensor_scalar_add(t.ct4[:], t.ct3[:], 1.0)
        self.vbar(ci)
        if v:
            ci = v.reciprocal(t.ct5[:], t.ct4[:])
        self.vbar(ci)
        if v:
            ci = v.tensor_scalar_mul(t.Cneg[:], t.ct5[:], -1.0)
        cd = self.fin(ci, 'cdn')
        self.w('vector', 'cdn', cd)

    def a3(self):
        t = self.t
        s = self.on('sync')
        self.w('sync', 'qkm', 2 * NFD * NCH)
        self.w('sync', 'ust', NFD * NCH)
        if s:
            s.dma_start(t.wqT[:], t.woT_h).then_inc(self.sems['wol'], 16)
        wol = self.inc('wol', 16)

        for c in range(NCH):
            s = self.on('sync')
            self.w('sync', 'pb3', NFD * (c - 1))
            if c < 2:
                self.w('sync', 'pbm', NFD * NCH)
            self.w('sync', 'ptsA' if c % 2 == 0 else 'ptsB', 16 * (c // 2 + 1))
            ptlsem = 'ptlA' if c % 2 == 0 else 'ptlB'
            if s:
                s.dma_start(t.piTc[:16, (c % 2) * CH:(c % 2 + 1) * CH],
                            t.pit_h[:, c * CH:(c + 1) * CH]).then_inc(self.sems[ptlsem], 16)
            ptl = self.inc(ptlsem, 16)

            te, v = self.on('tensor'), self.on('vector')
            for f in range(NFD):
                idx = c * NFD + f
                bank = t.pib[idx % 2]
                self.w('tensor', ptlsem, ptl)
                if idx == 0:
                    self.w('tensor', 'ust', NFD * NCH)
                self.w('tensor', 'osd', idx - 1)
                last = None
                if te:
                    last = te.matmul(bank[:], t.M2[:16, f * 128:(f + 1) * 128],
                                     t.piTc[:16, (c % 2) * CH:(c % 2 + 1) * CH],
                                     start=True, stop=True)
                pb3 = self.fin(last, 'pb3')
                self.w('vector', 'pb3', pb3)
                if c > 0 and f == 0:
                    self.w('vector', 'aom', 2 * NT * c)
                oi = None
                if v:
                    oi = v.scalar_tensor_tensor(t.oTbuf[:, f * CH:(f + 1) * CH],
                                                t.wT[:, f * NTOK + c * CH: f * NTOK + (c + 1) * CH],
                                                t.Cneg[:, f:f + 1], bank[:],
                                                ALU.mult, ALU.mult)
                self.fin(oi, 'osd')

            s = self.on('sync')
            if c == 0:
                self.w('sync', 'lnf', NTILES)
            for tt in range(NT):
                gidx = c * NT + tt
                self.w('sync', 'rev', 2 * (gidx - 1))
                if s:
                    s.dma_start(t.xbuf[tt % 2][:], t.x[gidx * 128:(gidx + 1) * 128, :]) \
                        .then_inc(self.sems['xl2A' if gidx % 2 == 0 else 'xl2B'], 16)
                self.inc('xl2A' if gidx % 2 == 0 else 'xl2B', 16)

            te = self.on('tensor')
            self.w('tensor', 'osd', self.cnt['osd'])
            if c == 0:
                self.w('tensor', 'wol', wol)
            for tt in range(NT):
                for nh in range(2):
                    aidx = (c * NT + tt) * 2 + nh
                    bank = t.aop[aidx % 3]
                    self.w('tensor', 'rev', aidx - 2)
                    last = None
                    if te:
                        for f in range(NFD):
                            last = te.matmul(bank[:],
                                             t.oTbuf[:, f * CH + tt * 128: f * CH + tt * 128 + 128],
                                             t.wqT[:, f * 1024 + nh * CH: f * 1024 + (nh + 1) * CH],
                                             start=(f == 0), stop=False)
                        last = te.matmul(bank[:], t.onesrow[:], t.b_bf[0:1, nh * CH:(nh + 1) * CH],
                                         start=False, stop=True)
                    self.fin(last, 'aom')

            v, a = self.on('vector'), self.on('scalar')
            for tt in range(NT):
                gidx = c * NT + tt
                self.w('vector', 'aom', (c * NT + tt) * 2 + 2)
                self.w('vector', 'xl2A' if gidx % 2 == 0 else 'xl2B', 16 * (gidx // 2 + 1))
                self.w('vector', 'rstA' if gidx % 2 == 0 else 'rstB', 16 * (gidx // 2))
                for nh in range(2):
                    aidx = (c * NT + tt) * 2 + nh
                    ri = None
                    if v:
                        ri = v.tensor_tensor(t.Rbuf[:, (tt % 2) * D + nh * CH:(tt % 2) * D + (nh + 1) * CH],
                                             t.aop[aidx % 3][:],
                                             t.xbuf[tt % 2][:, nh * CH:(nh + 1) * CH], ALU.add)
                    self.fin(ri, 'rev')
                rv = self.cnt['rev']
                self.w('scalar', 'rev', rv)
                if a:
                    a.dma_start(t.r_h[gidx * 128:(gidx + 1) * 128, :],
                                t.Rbuf[:, (tt % 2) * D:(tt % 2 + 1) * D]) \
                        .then_inc(self.sems['rstA' if gidx % 2 == 0 else 'rstB'], 16)
                self.inc('rstA' if gidx % 2 == 0 else 'rstB', 16)

    def f(self):
        t = self.t
        s, te, a, v = self.on('sync'), self.on('tensor'), self.on('scalar'), self.on('vector')

        self.w('sync', 'osd', NFD * NCH)
        self.w('sync', 'pb3', NFD * NCH)
        self.w('sync', 'rstA', 16 * (NTILES // 2))
        self.w('sync', 'rstB', 16 * (NTILES // 2))
        for pw in ['pw2A', 'pw2B', 'pw1A', 'pw1B', 'pw0A', 'pw0B']:
            self.w('sync', pw, self.cnt[pw])
        if s:
            s.dma_start(t.w2res[:], t.w2s_h).then_inc(self.sems['w2l'], 16)
        w2l = self.inc('w2l', 16)

        for c in range(NCH):
            self.w('sync', 'h1m', F1T * c)
            if s:
                src = t.xnt_h[:, :, c * CH:(c + 1) * CH].rearrange("f p tok -> p f tok")
                if c == 0:
                    s.wait_ge(self.sems['xns'], 32 * NCH)
                s.dma_start(t.xnl[:].rearrange("p (f tok) -> p f tok", f=NFD), src) \
                    .then_inc(self.sems['xnl'], 16)
            xnl = self.inc('xnl', 16)

            for fi in range(FPC):
                fg = c * FPC + fi
                par = fg % 2
                par3 = fg % 3
                fsem = ['wflA', 'wflB', 'wflC'][par3]
                self.w('sync', 'wcv', fg - 2)
                if fi < 2:
                    self.w('sync', 'wcv', (c - 1) * FPC + fi + 7)
                elif fi < 8:
                    self.w('sync', 'wcv', fg - 1)
                if s:
                    if fi < 8:
                        s.dma_start(t.u8p[par3][:], t.k0ts[fi]).then_inc(self.sems[fsem], 16)
                        s.dma_start(t.u81[par][:], t.k0bs[fi]).then_inc(self.sems[fsem], 16)
                    else:
                        s.dma_start(t.u8p[par3][:], t.k1s[fi - 8]).then_inc(self.sems[fsem], 16)
                self.inc(fsem, 32 if fi < 8 else 16)
                vd = self.on('vector')
                self.w('vector', fsem, self.cnt[fsem])
                if fg >= 2:
                    cp2, fip = (fg - 2) // FPC, (fg - 2) % FPC
                    if fip < 8:
                        self.w('vector', 'h1m', cp2 * F1T + 4 * (fip + 1))
                    else:
                        self.w('vector', 'h2m', cp2 * F2T + (fip - 8) + 1)
                ci = None
                if vd:
                    if fi < 8:
                        ci = vd.tensor_tensor(t.wbuf[par][:], t.u8p[par3][:], t.u81[par][:],
                                              ALU.subtract)
                    else:
                        ci = vd.tensor_scalar(t.wbuf[par][:], t.u8p[par3][:], 2.0, -1.0,
                                              ALU.mult, ALU.add)
                self.fin(ci, 'wcv')

            self.w('tensor', 'xnl', xnl)
            if c == 0:
                self.w('tensor', 'w2l', w2l)
            for m1 in range(F1T):
                fg = c * FPC + m1 // 4
                self.w('tensor', 'wcv', fg + 1)
                self.w('tensor', 'h1e', c * F1T + m1 - 3)
                bank = t.hb[m1 % 4]
                last = None
                if te:
                    for k in range(NFD):
                        last = te.matmul(bank[:],
                                         t.wbuf[fg % 2][:, (m1 % 4) * 1024 + k * 128:(m1 % 4) * 1024 + k * 128 + 128],
                                         t.xnl[:, k * CH:(k + 1) * CH],
                                         start=(k == 0), stop=(k == NFD - 1))
                h1m = self.fin(last, 'h1m')
                self.w('scalar', 'h1m', h1m)
                if c > 0 and m1 == 0:
                    self.w('scalar', 'h2m', F2T * c)
                ei = None
                if a:
                    ei = a.activation(t.h1t[:, m1 * CH:(m1 + 1) * CH], bank[:], AF.Relu,
                                      scale=t.scalsB[:, 0:1])
                self.fin(ei, 'h1e')

            self.w('tensor', 'h1e', F1T * (c + 1))
            for m2 in range(F2T):
                fg = c * FPC + 8 + m2
                self.w('tensor', 'wcv', fg + 1)
                self.w('tensor', 'h2e', c * F2T + m2 - 3)
                bank = t.hb[m2 % 4]
                last = None
                if te:
                    for k in range(F1T):
                        last = te.matmul(bank[:], t.wbuf[fg % 2][:, k * 128:(k + 1) * 128],
                                         t.h1t[:, k * CH:(k + 1) * CH],
                                         start=(k == 0), stop=(k == F1T - 1))
                h2m = self.fin(last, 'h2m')
                self.w('scalar', 'h2m', h2m)
                if c > 0 and m2 == 0:
                    self.w('scalar', 'ym', 2 * NT * c)
                ei = None
                if a:
                    ei = a.activation(t.h2t[:, m2 * CH:(m2 + 1) * CH], bank[:], AF.Relu,
                                      scale=t.scalsB[:, 1:2])
                self.fin(ei, 'h2e')

            for tt in range(NT):
                gidx = c * NT + tt
                self.w('sync', 'yev', 2 * (gidx - 1))
                if s:
                    s.dma_start(t.xbuf[tt % 2][:], t.r_h[gidx * 128:(gidx + 1) * 128, :]) \
                        .then_inc(self.sems['rldA' if gidx % 2 == 0 else 'rldB'], 16)
                self.inc('rldA' if gidx % 2 == 0 else 'rldB', 16)

            self.w('tensor', 'h2e', F2T * (c + 1))
            for tt in range(NT):
                for nh in range(2):
                    yidx = (c * NT + tt) * 2 + nh
                    bank = t.yb[yidx % 4]
                    self.w('tensor', 'yev', yidx - 3)
                    last = None
                    if te:
                        for k in range(F2T):
                            last = te.matmul(bank[:],
                                             t.h2t[:, k * CH + tt * 128: k * CH + tt * 128 + 128],
                                             t.w2res[:, k * 1024 + nh * CH: k * 1024 + (nh + 1) * CH],
                                             start=(k == 0), stop=(k == F2T - 1))
                    self.fin(last, 'ym')

                gidx = c * NT + tt
                self.w('vector', 'ym', (c * NT + tt) * 2 + 2)
                self.w('vector', 'rldA' if gidx % 2 == 0 else 'rldB', 16 * (gidx // 2 + 1))
                self.w('vector', 'ostA' if gidx % 2 == 0 else 'ostB', 16 * (gidx // 2))
                if c == 0 and tt == 0:
                    self.w('vector', 'rstA', 16 * (NTILES // 2))
                    self.w('vector', 'rstB', 16 * (NTILES // 2))
                for nh in range(2):
                    yidx = (c * NT + tt) * 2 + nh
                    yi = None
                    if v:
                        yi = v.scalar_tensor_tensor(t.Rbuf[:, (tt % 2) * D + nh * CH:(tt % 2) * D + (nh + 1) * CH],
                                                    t.yb[yidx % 4][:], t.scalsB[:, 2:3],
                                                    t.xbuf[tt % 2][:, nh * CH:(nh + 1) * CH],
                                                    ALU.mult, ALU.add)
                    self.fin(yi, 'yev')
                yv = self.cnt['yev']
                self.w('scalar', 'yev', yv)
                if a:
                    a.dma_start(t.out[gidx * 128:(gidx + 1) * 128, :],
                                t.Rbuf[:, (tt % 2) * D:(tt % 2 + 1) * D]) \
                        .then_inc(self.sems['ostA' if gidx % 2 == 0 else 'ostB'], 16)
                self.inc('ostA' if gidx % 2 == 0 else 'ostB', 16)

    def drain_dma_sems(self):
        for eng, sems in [('sync', ['ild', 'xld0', 'xld1', 'xld2', 'xld3', 'xl2A', 'xl2B',
                                    'ptlA', 'ptlB', 'wol', 'w2l',
                                    'wflA', 'wflB', 'wflC', 'xnl', 'rldA', 'rldB', 'ptsA', 'ptsB']),
                          ('scalar', ['wth', 'xns', 'rstA', 'rstB', 'ostA', 'ostB']),
                          ('gpsimd', ['dsc', 'pldA', 'pldB', 'pw2A', 'pw2B',
                                      'pw1A', 'pw1B', 'pw0A', 'pw0B', 'dbg'])]:
            for sm in sems:
                if self.cnt[sm] > 0:
                    self.w(eng, sm, self.cnt[sm])

    def tail(self):
        t = self.t
        g = self.on('gpsimd')
        if self.phases >= 4 and not self.debug:
            self.w('scalar', 'ostA', 16 * (NTILES // 2))
            self.w('scalar', 'ostB', 16 * (NTILES // 2))
            self.drain_dma_sems()
            return
        if self.debug:
            self.w('gpsimd', 'wve', 2 * NFD * NCH)
            self.w('gpsimd', 'nsq', NFD * NCH)
            self.w('gpsimd', 'rnd', 1)
            if self.phases >= 2:
                self.w('gpsimd', 'ptsA', 16 * (NCH // 2))
                self.w('gpsimd', 'ptsB', 16 * (NCH // 2))
                self.w('gpsimd', 'cdn', 1)
            if self.phases >= 3:
                self.w('gpsimd', 'rstA', 16 * (NTILES // 2))
                self.w('gpsimd', 'rstB', 16 * (NTILES // 2))
            if self.phases >= 4:
                self.w('gpsimd', 'ostA', 16 * (NTILES // 2))
                self.w('gpsimd', 'ostB', 16 * (NTILES // 2))
            nd = 0
            if self.phases < 4:
                if g:
                    g.dma_start(t.dwT, t.wT[:, 0:2048]).then_inc(self.sems['dbg'], 16)
                nd += 16
            if g:
                g.dma_start(t.dnsq, t.nsqacc[:]).then_inc(self.sems['dbg'], 16)
                g.dma_start(t.drn2, t.rn2[:]).then_inc(self.sems['dbg'], 16)
                g.dma_start(t.dxn, t.xnt_h[0, :, 0:1024]).then_inc(self.sems['dbg'], 16)
            nd += 3 * 16
            if self.phases >= 2:
                if g:
                    g.dma_start(t.dpit, t.pit_h[:, 0:1024]).then_inc(self.sems['dbg'], 16)
                    g.dma_start(t.dC, t.Cneg[:]).then_inc(self.sems['dbg'], 16)
                nd += 2 * 16
            if self.phases >= 3:
                if g:
                    g.dma_start(t.dR[0:128, :], t.r_h[0:128, :]).then_inc(self.sems['dbg'], 16)
                    g.dma_start(t.dR[128:256, :], t.r_h[4096:4224, :]).then_inc(self.sems['dbg'], 16)
                nd += 2 * 16
            if self.phases >= 4:
                if g:
                    g.dma_start(t.dh1, t.h1t[:]).then_inc(self.sems['dbg'], 16)
                    g.dma_start(t.dh2, t.h2t[:]).then_inc(self.sems['dbg'], 16)
                    g.dma_start(t.dw0, t.w0s_h[0]).then_inc(self.sems['dbg'], 16)
                    g.dma_start(t.dw1, t.w1s_h[0]).then_inc(self.sems['dbg'], 16)
                    g.dma_start(t.dw2, t.w2s_h[:, 0:4096]).then_inc(self.sems['dbg'], 16)
                nd += 5 * 16
            self.inc('dbg', nd)
            self.w('gpsimd', 'dbg', nd)
        else:
            if self.phases >= 3:
                self.w('scalar', 'rstA', 16 * (NTILES // 2))
                self.w('scalar', 'rstB', 16 * (NTILES // 2))
            else:
                self.w('scalar', 'wve', 2 * NFD * NCH)
        self.drain_dma_sems()


class T:
    pass


def build(phases=4, debug=False):
    nc = bass.Bass("TRN2", target_bir_lowering=False, detect_race_conditions=False)
    t = T()

    t.x = nc.dram_tensor("x", [NTOK, D], F32, kind="ExternalInput").ap()
    t.wqkv = nc.dram_tensor("wqkv", [D, D], F32, kind="ExternalInput").ap()
    t.temp = nc.dram_tensor("temp", [16, 1], F32, kind="ExternalInput").ap()
    t.wout = nc.dram_tensor("wout", [D, D], F32, kind="ExternalInput").ap()
    t.bout = nc.dram_tensor("bout", [1, D], F32, kind="ExternalInput").ap()
    t.k1s = nc.dram_tensor("k1s", [32, 128, 4096], U8, kind="ExternalInput").ap()
    t.k0ts = nc.dram_tensor("k0ts", [8, 128, 4096], U8, kind="ExternalInput").ap()
    t.k0bs = nc.dram_tensor("k0bs", [8, 128, 4096], U8, kind="ExternalInput").ap()
    t.k2 = nc.dram_tensor("k2", [4096, 1024], U8, kind="ExternalInput").ap()
    t.s0 = nc.dram_tensor("s0", [1, 1], F32, kind="ExternalInput").ap()
    t.s1 = nc.dram_tensor("s1", [1, 1], F32, kind="ExternalInput").ap()
    t.s2 = nc.dram_tensor("s2", [1, 1], F32, kind="ExternalInput").ap()
    t.out = nc.dram_tensor("out", [NTOK, D], F32, kind="ExternalOutput").ap()

    t.xnt_h = nc.dram_tensor("xnt_h", [NFD, 128, NTOK], BF16, kind="Internal").ap()
    t.r_h = nc.dram_tensor("r_h", [NTOK, D], F32, kind="Internal").ap()
    t.pit_h = nc.dram_tensor("pit_h", [16, NTOK], BF16, kind="Internal").ap()
    t.w2s_h = nc.dram_tensor("w2s_h", [128, F2T * 1024], BF16, kind="Internal").ap()
    t.der_h = nc.dram_tensor("der_h", [1, 8], F32, kind="Internal").ap()
    t.s_h = nc.dram_tensor("s_h", [16, 1], F32, kind="Internal").ap()
    t.woT_h = nc.dram_tensor("woT_h", [128, NFD * 1024], BF16, kind="Internal").ap()

    if debug:
        t.dxn = nc.dram_tensor("dxn", [128, 1024], BF16, kind="ExternalOutput").ap()
        t.dwT = nc.dram_tensor("dwT", [128, 2048], BF16, kind="ExternalOutput").ap()
        t.dnsq = nc.dram_tensor("dnsq", [128, 128], F32, kind="ExternalOutput").ap()
        t.drn2 = nc.dram_tensor("drn2", [128, 8], F32, kind="ExternalOutput").ap()
        if phases >= 2:
            t.dpit = nc.dram_tensor("dpit", [16, 1024], BF16, kind="ExternalOutput").ap()
            t.dC = nc.dram_tensor("dC", [128, 8], F32, kind="ExternalOutput").ap()
        if phases >= 3:
            t.dR = nc.dram_tensor("dR", [256, D], F32, kind="ExternalOutput").ap()
        if phases >= 4:
            t.dh1 = nc.dram_tensor("dh1", [128, F1T * CH], BF16, kind="ExternalOutput").ap()
            t.dh2 = nc.dram_tensor("dh2", [128, F2T * CH], BF16, kind="ExternalOutput").ap()
            t.dw0 = nc.dram_tensor("dw0", [128, 4096], BF16, kind="ExternalOutput").ap()
            t.dw1 = nc.dram_tensor("dw1", [128, 4096], BF16, kind="ExternalOutput").ap()
            t.dw2 = nc.dram_tensor("dw2", [128, 4096], BF16, kind="ExternalOutput").ap()

    sb = nc.alloc_sbuf_tensor
    t.zv = sb("zv", [128, 1], F32)
    t.ov = sb("ov", [128, 1], F32)
    t.epsv = sb("epsv", [128, 1], F32)
    t.onesrow = sb("onesrow", [1, 128], BF16)
    t.ident = sb("ident", [128, 128], BF16)
    t.E = sb("E", [128, 128], BF16)
    t.M2 = sb("M2", [16, NFD * 128], BF16)
    t.tempB = sb("tempB", [128, 16], F32)
    t.scalsB = sb("scalsB", [128, 8], F32)
    t.svec = sb("svec", [1, 4], F32)
    t.der = sb("der", [1, 8], F32)
    t.rn2 = sb("rn2", [128, NFD], F32)
    t.rn2a = sb("rn2a", [128, NFD], F32)
    t.rn2b = sb("rn2b", [128, NFD], F32)
    t.SB = sb("SBt", [128, NFD], F32)
    t.SBfull = sb("SBfull", [128, 16], F32)
    t.Cneg = sb("Cneg", [128, NFD], F32)
    t.nsqacc = sb("nsqacc", [128, 128], F32)
    t.Uacc = sb("Uacc", [128, 128], F32)
    t.Ufin = sb("Ufin", [128, NFD], F32)
    t.Sacc = sb("Sacc", [16, 16], F32)
    t.Sfin = sb("Sfin", [16, 1], F32)
    t.ct1 = sb("ct1", [128, NFD], F32)
    t.ct2 = sb("ct2", [128, NFD], F32)
    t.ct3 = sb("ct3", [128, NFD], F32)
    t.ct4 = sb("ct4", [128, NFD], F32)
    t.ct5 = sb("ct5", [128, NFD], F32)
    t.b_bf = sb("b_bf", [1, D], BF16)
    for nm in ["lnsum", "lnm", "lnmsq", "lnt1", "ssqs", "lnvar", "lnsd", "lnrstd"]:
        setattr(t, nm, sb(nm, [128, 2 * NT], F32))
    t.ltbuf = sb("ltbuf", [128, 128], F32)
    t.nmx4 = sb("nmx4", [128, 8], F32)
    t.ebuf = sb("ebuf", [128, 128], F32)
    t.ebuf2 = sb("ebuf2", [128, 128], F32)
    t.s4 = sb("s4", [128, 8], F32)
    t.rs4 = sb("rs4", [128, 8], F32)
    t.pinb = sb("pinb", [128, 128], BF16)
    t.lgT = sb("lgT", [16, 2 * CH], BF16)
    t.Ufin2 = sb("Ufin2", [128, NFD], F32)
    t.junk = sb("junk", [128, D], BF16)
    t.junkb = sb("junkb", [128, CH], BF16)
    t.wn2 = sb("wn2", [128, 2 * CH], BF16)
    xr = sb("xr", [128, 16384], U8)
    xrbase = nc.lookup_mloc(xr).addr
    t.xbuf4 = [nc.alloc_sbuf_tensor_at(f"xb4_{i}", [128, D], F32, offset=xrbase + 4096 * i)
               for i in range(4)]
    t.Rbuf = nc.alloc_sbuf_tensor_at("Rbuf", [128, 2 * D], F32, offset=xrbase + 8192)
    t.xbuf = t.xbuf4[:2]
    t.piTc = sb("piTc", [16, 2 * CH], BF16)

    ARENA = 178176
    arena = nc.alloc_sbuf_tensor("arena", [128, ARENA], U8)
    abase = nc.lookup_mloc(arena).addr

    def at(name, shape, dtype, off):
        return nc.alloc_sbuf_tensor_at(name, shape, dtype, offset=abase + off)

    t.wT = at("wT", [128, NFD * NTOK], BF16, 0)
    t.wqT = at("wqT", [128, NFD * 1024], BF16, 131072)
    t.xnTcH = [at("xnTcH0", [128, NFD * 256], BF16, 147456),
               at("xnTcH1", [128, NFD * 256], BF16, 151552)]
    t.oTbuf = at("oTbuf", [128, NFD * CH], BF16, 147456)
    t.xnbuf = at("xnbuf", [128, 2 * D], BF16, 155648)
    t.wq32 = at("wq32", [128, NFD * 1024], F32, 0)
    t.wo32 = at("wo32", [128, NFD * 1024], F32, 32768)
    t.wqb = at("wqb", [128, NFD * 1024], BF16, 65536)
    t.wob = at("wob", [128, NFD * 1024], BF16, 81920)
    t.b32 = at("b32", [1, D], F32, 98304)
    t.ujunk = at("ujunk", [128, CH], BF16, 139264)
    t.wn2f = [at("wn2fA", [128, NFD * CH], BF16, 147456),
              at("wn2fB", [128, NFD * CH], BF16, 131072)]
    t.pA2 = [at("pA2a", [128, 4096], U8, 159744), at("pA2b", [128, 4096], U8, 163840)]
    t.pD2 = [at("pD2a", [128, 1024], BF16, 167936), at("pD2b", [128, 1024], BF16, 169984)]
    t.u8p = [at("u80a", [128, 4096], U8, 159744), at("u80b", [128, 4096], U8, 163840),
             at("u80c", [128, 4096], U8, 155648)]
    t.u81 = [at("u81a", [128, 4096], U8, 167936), at("u81b", [128, 4096], U8, 172032)]
    t.w2res = at("w2res", [128, F2T * 1024], BF16, 0)
    t.wbuf = [at("wbufA", [128, 4096], BF16, 65536), at("wbufB", [128, 4096], BF16, 73728)]
    t.h1t = at("h1t", [128, F1T * CH], BF16, 81920)
    t.h2t = at("h2t", [128, F2T * CH], BF16, 114688)
    t.xnl = at("xnl", [128, NFD * CH], BF16, 147456)

    psnap = nc.psum_base
    t.ptr = [nc.alloc_psum_tensor("ptr0", [128, 1024], BF16),
             nc.alloc_psum_tensor("ptr1", [128, 1024], BF16)]
    t.qkvp = [nc.alloc_psum_tensor(f"qkvp{i}", [128, CH], F32) for i in range(6)]
    nc.psum_base = psnap
    t.lgp = [nc.alloc_psum_tensor(f"lgp{i}", [16, CH], F32) for i in range(2)]
    t.lgnp = [nc.alloc_psum_tensor(f"lgnp{i}", [128, 64], BF16) for i in range(2)]
    t.pitp = [nc.alloc_psum_tensor(f"pitp{i}", [16, CH], BF16) for i in range(2)]
    t.pibu = [nc.alloc_psum_tensor(f"pibu{i}", [128, CH], F32) for i in range(2)]
    nc.psum_base = psnap
    t.pib = [nc.alloc_psum_tensor(f"pib{i}", [128, CH], F32) for i in range(2)]
    t.aop = [nc.alloc_psum_tensor(f"aop{i}", [128, CH], F32) for i in range(3)]
    nc.psum_base = psnap
    t.hb = [nc.alloc_psum_tensor(f"hb{i}", [128, CH], F32) for i in range(4)]
    t.yb = [nc.alloc_psum_tensor(f"yb{i}", [128, CH], F32) for i in range(4)]

    nc.const_aps.register(F32, 0.0, t.zv[:])
    nc.const_aps.register(F32, 1.0, t.ov[:])

    semnames = ['vbr', 'ini', 'inv', 'ild', 'dsc', 'wtt', 'wts', 'wth', 'wtb', 'm2t', 'm2v',
                'pldA', 'pldB', 'pw2A', 'pw2B', 'pw1A', 'pw1B', 'pw0A', 'pw0B',
                'xld0', 'xld1', 'xld2', 'xld3', 'sqd', 'lnA', 'lnB', 'lnr', 'lnf', 'trp', 'xev', 'xns',
                'qkm', 'wve', 'nsq', 'rnd', 'rns', 'epv', 'lsg',
                'wnv', 'emm', 'lgm', 'lgv', 'lgt', 'pnv', 'sfa', 'sfb', 'pnd',
                'ptm', 'ptv', 'ptsA', 'ptsB', 'pbm', 'ust', 'ssv', 'ssw', 'cdn',
                'wol', 'ptlA', 'ptlB', 'pb3', 'osd', 'aom', 'xl2A', 'xl2B', 'rev', 'rstA', 'rstB',
                'w2l', 'wflA', 'wflB', 'wflC', 'wcv', 'xnl', 'h1m', 'h1e', 'h2m', 'h2e', 'ym', 'yev',
                'rldA', 'rldB', 'ostA', 'ostB', 'dbg']
    stk = ExitStack()
    sems = {n: stk.enter_context(nc.semaphore(n)) for n in semnames}

    builder = B(nc, sems, t, phases, debug)
    with nc.Block() as block:
        @block.sync
        def _(sync):
            builder.run('sync', sync)

        @block.tensor
        def _(tensor):
            builder.run('tensor', tensor)

        @block.vector
        def _(vector):
            builder.run('vector', vector)

        @block.scalar
        def _(scalar):
            builder.run('scalar', scalar)

        @block.gpsimd
        def _(gpsimd):
            builder.run('gpsimd', gpsimd)

    stk.close()
    return nc


def kernel(L0_attn_x, L0_attn_W_qkv, L0_attn_temp, L0_attn_W_out, L0_attn_b_out,
           L0_ffn_k0, L0_ffn_k1, L0_ffn_k2, L0_ffn_s0, L0_ffn_s1, L0_ffn_s2):
    x = np.ascontiguousarray(np.asarray(L0_attn_x, dtype=np.float32))
    shared = {
        "wqkv": np.ascontiguousarray(np.asarray(L0_attn_W_qkv, dtype=np.float32)),
        "temp": np.ascontiguousarray(np.asarray(L0_attn_temp, dtype=np.float32).reshape(16, 1)),
        "wout": np.ascontiguousarray(np.asarray(L0_attn_W_out, dtype=np.float32)),
        "bout": np.ascontiguousarray(np.asarray(L0_attn_b_out, dtype=np.float32).reshape(1, D)),
        "k1s": np.ascontiguousarray(
            np.asarray(L0_ffn_k1).astype(np.uint8).reshape(32, 128, 32, 128)
            .transpose(2, 1, 0, 3).reshape(32, 128, 4096)),
        "k0ts": np.ascontiguousarray(
            np.asarray(L0_ffn_k0).astype(np.uint8)[:1024].reshape(8, 128, 8, 4, 128)
            .transpose(2, 1, 3, 0, 4).reshape(8, 128, 4096)),
        "k0bs": np.ascontiguousarray(
            np.asarray(L0_ffn_k0).astype(np.uint8)[1024:].reshape(8, 128, 8, 4, 128)
            .transpose(2, 1, 3, 0, 4).reshape(8, 128, 4096)),
        "k2": np.ascontiguousarray(np.asarray(L0_ffn_k2).astype(np.uint8)),
        "s0": np.ascontiguousarray(np.asarray(L0_ffn_s0, dtype=np.float32).reshape(1, 1)),
        "s1": np.ascontiguousarray(np.asarray(L0_ffn_s1, dtype=np.float32).reshape(1, 1)),
        "s2": np.ascontiguousarray(np.asarray(L0_ffn_s2, dtype=np.float32).reshape(1, 1)),
    }
    nc = build()
    in_maps = [dict(shared, x=np.ascontiguousarray(x[b])) for b in range(8)]
    res = run_bass_kernel_spmd(nc, in_maps, core_ids=list(range(8)))
    return np.stack([r["out"] for r in res.results], axis=0)


# version 21
# speedup vs baseline: 1.0130x; 1.0130x over previous
import numpy as np
from collections import defaultdict
from contextlib import ExitStack

import concourse.bass as bass
import concourse.mybir as mybir
from concourse.bass_utils import run_bass_kernel_spmd

F32 = mybir.dt.float32
BF16 = mybir.dt.bfloat16
U8 = mybir.dt.uint8
AF = mybir.ActivationFunctionType
ALU = mybir.AluOpType
AX = mybir.AxisListType
bcast = bass.broadcast_tensor_aps

NTOK = 8192
D = 1024
NCH = 16
CH = 512
NT = 4
NFD = 8
F1T = 32
F2T = 32
NTILES = NCH * NT
FPC = 8 + F2T


class B:
    def __init__(self, nc, sems, t, phases, debug):
        self.nc, self.sems, self.t = nc, sems, t
        self.phases, self.debug = phases, debug

    def run(self, eng, handle):
        self.eng, self.h = eng, handle
        self.cnt = defaultdict(int)
        self.emit()

    def on(self, eng):
        return self.h if self.eng == eng else None

    def w(self, eng, sem, val):
        if self.eng == eng and val > 0:
            self.h.wait_ge(self.sems[sem], val)

    def inc(self, sem, n=1):
        self.cnt[sem] += n
        return self.cnt[sem]

    def fin(self, inst, sem, n=1):
        if inst is not None:
            inst.then_inc(self.sems[sem], n)
        return self.inc(sem, n)

    def vbar(self, inst):
        val = self.fin(inst, 'vbr')
        self.w('vector', 'vbr', val)

    def emit(self):
        self.init()
        self.prep()
        self.a1()
        if self.phases >= 2:
            self.a2()
        if self.phases >= 3:
            self.a3()
        if self.phases >= 4:
            self.f()
        self.tail()

    def trans_1024(self, src, dst, tag_t, tag_e):
        t = self.t
        te, a = self.on('tensor'), self.on('scalar')
        for arow in range(8):
            bank = t.ptr[arow % 2]
            self.w('tensor', tag_e, self.cnt[tag_e] - 1)
            last = None
            if te:
                for jb in range(8):
                    last = te.transpose(bank[:, jb * 128:(jb + 1) * 128],
                                        src[:, arow * 1024 + jb * 128: arow * 1024 + jb * 128 + 128],
                                        t.ident[:])
            trd = self.fin(last, tag_t)
            self.w('scalar', tag_t, trd)
            ei = None
            if a:
                outap = dst[:].rearrange("p (j i) -> p j i", j=8)[:, :, arow * 128:arow * 128 + 128]
                ei = a.copy(outap, bank[:].rearrange("p (j i) -> p j i", j=8))
            self.fin(ei, tag_e)

    def init(self):
        t = self.t
        g, s, v = self.on('gpsimd'), self.on('sync'), self.on('vector')
        te, a = self.on('tensor'), self.on('scalar')

        last = None
        if g:
            g.memset(t.zv[:], 0.0)
            g.memset(t.ov[:], 1.0)
            g.memset(t.epsv[:], 1e-5)
            g.memset(t.onesrow[:], 1.0)
            g.memset(t.ident[:], 0.0)
            g.affine_select(out=t.ident[:], in_=t.ident[:], compare_op=ALU.not_equal,
                            fill=1.0, base=0, pattern=[[-1, 128]], channel_multiplier=1)
            g.memset(t.E[:], 0.0)
            for f in range(NFD):
                g.memset(t.E[0:64, f * 16 + 2 * f: f * 16 + 2 * f + 1], 1.0)
                last = g.memset(t.E[64:128, f * 16 + 2 * f + 1: f * 16 + 2 * f + 2], 1.0)
        ini = self.fin(last, 'ini')

        if s:
            s.dma_start(t.wq32[:], t.wqkv.rearrange("(a p) f -> p a f", p=128)).then_inc(self.sems['ild'], 16)
            s.dma_start(t.wo32[:], t.wout.rearrange("(a p) f -> p a f", p=128)).then_inc(self.sems['ild'], 16)
            s.dma_start(t.b32[:], t.bout).then_inc(self.sems['ild'], 16)
            for j, st in enumerate([t.s0, t.s1, t.s2]):
                s.dma_start(t.svec[:, j:j + 1], st).then_inc(self.sems['ild'], 16)
            s.dma_start(t.tempB[:], bass.AP(t.temp.tensor, 0, [[0, 128], [1, 16]])).then_inc(self.sems['ild'], 16)
        ild = self.inc('ild', 16 * 7)

        self.w('vector', 'ild', ild)
        last = None
        if v:
            v.memset(t.der[:], 0.0)
            v.tensor_copy(t.wqb[:], t.wq32[:])
            v.tensor_copy(t.wob[:], t.wo32[:])
            v.tensor_copy(t.b_bf[:], t.b32[:])
            v.tensor_tensor(t.der[:, 0:1], t.svec[:, 0:1], t.svec[:, 0:1], ALU.add)
            v.tensor_tensor(t.der[:, 1:2], t.svec[:, 1:2], t.svec[:, 1:2], ALU.add)
            last = v.tensor_tensor(t.der[:, 2:3], t.svec[:, 2:3], t.svec[:, 2:3], ALU.add)
        inv = self.fin(last, 'inv')

        self.w('gpsimd', 'inv', inv)
        if g:
            g.dma_start(t.der_h, t.der[:]).then_inc(self.sems['dsc'], 16)
        d1 = self.inc('dsc', 16)
        self.w('gpsimd', 'dsc', d1)
        if g:
            g.dma_start(t.scalsB[:], bass.AP(t.der_h.tensor, 0, [[0, 128], [1, 8]])).then_inc(self.sems['dsc'], 16)
        self.inc('dsc', 16)

        self.w('tensor', 'inv', inv)
        self.w('tensor', 'ini', ini)
        self.trans_1024(t.wob, t.wqT, 'wtt', 'wts')
        wtsv = self.cnt['wts']
        self.w('scalar', 'wts', wtsv)
        ei = None
        if a:
            ei = a.dma_start(t.woT_h, t.wqT[:])
            ei.then_inc(self.sems['wth'], 16)
        wth = self.inc('wth', 16)
        self.w('tensor', 'wth', wth)
        self.trans_1024(t.wqb, t.wqT, 'wtt', 'wts')
        wtb = self.fin(te.nop() if te else None, 'wtb')

        for f in range(NFD):
            self.w('tensor', 'm2v', self.cnt['m2v'])
            last = None
            if te:
                last = te.transpose(t.pitp[0][:16, 0:128], t.E[:, f * 16:(f + 1) * 16], t.ident[:])
            m2t = self.fin(last, 'm2t')
            self.w('vector', 'm2t', m2t)
            mi = None
            if v:
                mi = v.tensor_copy(t.M2[:16, f * 128:(f + 1) * 128], t.pitp[0][:16, 0:128])
            self.fin(mi, 'm2v')

    def prep(self):
        t = self.t
        g = self.on('gpsimd')
        self.w('gpsimd', 'dsc', self.cnt['dsc'])

        def pldsem(kt):
            return 'pldA' if kt % 2 == 0 else 'pldB'

        def k2load(kt):
            if g:
                g.dma_start(t.pA2[kt % 2][:, 0:1024],
                            t.k2.rearrange("(a p) f -> p a f", p=128)[:, kt, :]) \
                    .then_inc(self.sems[pldsem(kt)], 16)
            self.inc(pldsem(kt), 16)

        k2load(0)
        k2load(1)
        for kc in range(F2T):
            pc = kc % 2
            wsem = 'pw2A' if pc == 0 else 'pw2B'
            self.w('gpsimd', pldsem(kc), self.cnt[pldsem(kc)])
            self.w('gpsimd', wsem, self.cnt[wsem])
            if g:
                g.tensor_scalar(t.pD2[pc][:], t.pA2[pc][:, 0:1024], 2.0, -1.0,
                                ALU.mult, ALU.add)
                g.dma_start(t.w2s_h[:, kc * 1024:(kc + 1) * 1024], t.pD2[pc][:]) \
                    .then_inc(self.sems[wsem], 16)
            self.inc(wsem, 16)
            if kc + 2 < F2T:
                k2load(kc + 2)


    def a1(self):
        t = self.t
        self.w('scalar', 'ini', 1)
        for c in range(NCH):
            cp = c % 2
            s, v, a, te = self.on('sync'), self.on('vector'), self.on('scalar'), self.on('tensor')
            sl = slice(cp * NT, cp * NT + NT)
            for tt in range(NT):
                gidx = c * NT + tt
                self.w('sync', 'lnf', (c - 1) * NT + tt + 1)
                xsem = f'xld{tt}'
                if s:
                    s.dma_start(t.xbuf4[tt][:], t.x[gidx * 128:(gidx + 1) * 128, :]) \
                        .then_inc(self.sems[xsem], 16)
                xv = self.inc(xsem, 16)
                self.w('vector', xsem, xv)
                if v:
                    lastred = v.tensor_reduce(t.lnsum[:, cp * NT + tt:cp * NT + tt + 1],
                                              t.xbuf4[tt][:], AX.X, ALU.add)
                else:
                    lastred = None
                self.w('scalar', xsem, xv)
                self.w('scalar', 'lnA', c - 1)
                sqi = None
                if a:
                    sqi = a.activation(t.junk[:], t.xbuf4[tt][:], AF.Square,
                                       accum_out=t.ssqs[:, cp * NT + tt:cp * NT + tt + 1])
                self.fin(sqi, 'sqd')
            self.vbar(lastred)
            self.w('vector', 'sqd', self.cnt['sqd'])
            ri = None
            if v:
                v.tensor_scalar_mul(t.lnm[:, sl], t.lnsum[:, sl], 1.0 / D)
                ri = v.tensor_scalar_mul(t.lnt1[:, sl], t.ssqs[:, sl], 1.0 / D)
            self.vbar(ri)
            if v:
                ri = v.tensor_tensor(t.lnmsq[:, sl], t.lnm[:, sl], t.lnm[:, sl], ALU.mult)
            self.vbar(ri)
            lnA = None
            if v:
                lnA = v.tensor_tensor(t.lnvar[:, sl], t.lnt1[:, sl], t.lnmsq[:, sl], ALU.subtract)
            va = self.fin(lnA, 'lnA')
            self.w('scalar', 'lnA', va)
            sdi = None
            if a:
                sdi = a.activation(t.lnsd[:, sl], t.lnvar[:, sl], AF.Sqrt, bias=t.epsv[:])
            vb = self.fin(sdi, 'lnB')
            self.w('vector', 'lnB', vb)
            ri = None
            if v:
                ri = v.reciprocal(t.lnrstd[:, sl], t.lnsd[:, sl])
            rr = self.fin(ri, 'lnr')
            self.w('vector', 'lnr', rr)
            for tt in range(NT):
                gidx = c * NT + tt
                par = gidx % 2
                self.w('vector', 'trp', gidx - 1)
                xni = None
                if v:
                    xni = v.tensor_scalar(t.xnbuf[:, par * D:(par + 1) * D], t.xbuf4[tt][:],
                                          t.lnm[:, cp * NT + tt:cp * NT + tt + 1],
                                          t.lnrstd[:, cp * NT + tt:cp * NT + tt + 1],
                                          ALU.subtract, ALU.mult)
                lnf = self.fin(xni, 'lnf')

                self.w('tensor', 'lnf', lnf)
                self.w('tensor', 'xev', gidx - 1)
                if gidx == 0:
                    self.w('tensor', 'wts', 16)
                last = None
                if te:
                    for f in range(NFD):
                        last = te.transpose(t.ptr[par][:, f * 128:(f + 1) * 128],
                                            t.xnbuf[:, par * D + f * 128: par * D + f * 128 + 128],
                                            t.ident[:])
                trp = self.fin(last, 'trp')

                h = tt // 2
                hidx = 2 * c + h
                slot = hidx % 2
                self.w('scalar', 'trp', trp)
                if tt % 2 == 0 and hidx >= 2:
                    self.w('scalar', 'qkm', NFD * (hidx - 1))
                    self.w('scalar', 'xns', 16 * (hidx - 1))
                ev = None
                if a:
                    outap = t.xnTcH[slot][:].rearrange("p (f tok) -> p f tok", f=NFD)[:, :, (tt % 2) * 128:(tt % 2) * 128 + 128]
                    ev = a.copy(outap, t.ptr[par][:].rearrange("p (f i) -> p f i", f=NFD))
                self.fin(ev, 'xev')
                if tt % 2 == 0:
                    continue
                self.w('scalar', 'xev', self.cnt['xev'])
                if a:
                    dst = t.xnt_h[:, :, c * CH + h * 256: c * CH + (h + 1) * 256].rearrange("f p tok -> p f tok")
                    a.dma_start(dst, t.xnTcH[slot][:].rearrange("p (f tok) -> p f tok", f=NFD)) \
                        .then_inc(self.sems['xns'], 16)
                self.inc('xns', 16)
                self.w('tensor', 'xev', self.cnt['xev'])
                if hidx == 0:
                    self.w('tensor', 'wtb', 1)
                    self.w('tensor', 'm2v', NFD)
                for i in range(NFD):
                    ghidx = hidx * NFD + i
                    bank = t.qkvp[ghidx % 6]
                    self.w('tensor', 'wve', ghidx - 5)
                    last = None
                    if te:
                        for j in range(NFD):
                            last = te.matmul(bank[:, 0:256],
                                             t.wqT[:, j * 1024 + i * 128: j * 1024 + i * 128 + 128],
                                             t.xnTcH[slot][:, j * 256:(j + 1) * 256],
                                             start=(j == 0), stop=(j == NFD - 1))
                    qk = self.fin(last, 'qkm')
                    self.w('scalar', 'qkm', qk)
                    wv = None
                    if a:
                        wv = a.copy(t.wT[:, i * NTOK + c * CH + h * 256: i * NTOK + c * CH + h * 256 + 256],
                                    bank[:, 0:256])
                    self.fin(wv, 'wve')
                    if h == 1:
                        nsq = None
                        if a:
                            nsq = a.activation(t.junkb[:], t.wT[:, i * NTOK + c * CH: i * NTOK + (c + 1) * CH],
                                               AF.Square, accum_out=t.nsqacc[:, i * NCH + c: i * NCH + c + 1])
                        self.fin(nsq, 'nsq')

        v = self.on('vector')
        self.w('vector', 'nsq', NFD * NCH)
        ri = None
        if v:
            ri = v.tensor_reduce(t.rn2a[:], t.nsqacc[:].rearrange("p (f c) -> p f c", f=NFD), AX.X, ALU.add)
        self.vbar(ri)
        if v:
            ri = v.tensor_scalar_max(t.rn2b[:], t.rn2a[:], 1e-24)
        self.vbar(ri)
        if v:
            ri = v.reciprocal(t.rn2[:], t.rn2b[:])
        rn = self.fin(ri, 'rnd')
        self.w('vector', 'rnd', rn)
        ri = None
        if v:
            for f in range(NFD):
                ri = v.tensor_scalar_mul(t.E[:, f * 16:(f + 1) * 16], t.E[:, f * 16:(f + 1) * 16],
                                         t.rn2[:, f:f + 1])
        self.vbar(ri)
        ni = self.on('vector').nop() if self.on('vector') else None
        self.fin(ni, 'epv')

    def a2_front1a(self, c):
        t = self.t
        cp = c % 2
        v, te = self.on('vector'), self.on('tensor')
        a = self.on('scalar')
        self.w('scalar', 'emm', c - 1)
        self.w('scalar', 'ust', NFD * (c - 1))
        if c == 0:
            self.w('scalar', 'qkm', 2 * NFD * NCH)
            self.w('scalar', 'xns', 32 * NCH)
        wni = None
        for f in range(NFD):
            if a:
                wni = a.activation(t.wn2f[cp][:, f * CH:(f + 1) * CH],
                                   t.wT[:, f * NTOK + c * CH: f * NTOK + (c + 1) * CH],
                                   AF.Square)
        wn = self.fin(wni, 'wnv')
        self.w('tensor', 'wnv', wn)
        if c == 0:
            self.w('tensor', 'm2v', NFD)
            self.w('tensor', 'epv', 1)
        self.w('tensor', 'lgv', c - 1)
        last = None
        if te:
            for f in range(NFD):
                last = te.matmul(t.lgp[cp][:16, :], t.E[:, f * 16:(f + 1) * 16],
                                 t.wn2f[cp][:, f * CH:(f + 1) * CH],
                                 start=(f == 0), stop=(f == NFD - 1))
        self.fin(last, 'emm')

    def a2_front1b(self, c):
        t = self.t
        cp = c % 2
        v, te = self.on('vector'), self.on('tensor')
        self.w('vector', 'emm', c + 1)
        self.w('vector', 'lgt', c - 1)
        lv = None
        if v:
            lv = v.tensor_copy(t.lgT[:16, cp * CH:(cp + 1) * CH], t.lgp[cp][:16, :])
        lgv = self.fin(lv, 'lgv')
        self.w('tensor', 'lgv', lgv)
        self.w('tensor', 'pnv', c - 1)
        last = None
        if te:
            for q in range(NT):
                last = te.transpose(t.lgnp[cp][:, q * 16:(q + 1) * 16],
                                    t.lgT[:16, cp * CH + q * 128: cp * CH + (q + 1) * 128],
                                    t.ident[:16, 0:16])
        self.fin(last, 'lgt')

    def a2_front2(self, c):
        t = self.t
        cp = c % 2
        v, te, a = self.on('vector'), self.on('tensor'), self.on('scalar')
        self.w('vector', 'lgt', c + 1)
        lt3 = t.ltbuf[:, cp * 64:(cp + 1) * 64].rearrange("p (q h) -> p q h", q=NT)
        lti = None
        if v:
            ln3 = t.lgnp[cp][:].rearrange("p (q h) -> p q h", q=NT)
            tb3 = t.tempB[:].rearrange("p (o h) -> p o h", o=1)
            ln3b, tb3b = bcast(ln3, tb3)
            lti = v.tensor_tensor(lt3, ln3b, tb3b, ALU.mult)
        self.fin(lti, 'pnv')
        sfi = None
        if v:
            sfi = v.tensor_reduce(t.nmx4[:, cp * 4:(cp + 1) * 4], lt3, AX.X, ALU.max, negate=True)
        self.vbar(sfi)
        if v:
            nm3 = t.nmx4[:, cp * 4:(cp + 1) * 4].rearrange("p (q o) -> p q o", o=1)
            lt3b, nm3b = bcast(lt3, nm3)
            sfi = v.tensor_tensor(t.ebuf[:, cp * 64:(cp + 1) * 64].rearrange("p (q h) -> p q h", q=NT),
                                  lt3b, nm3b, ALU.add)
        sfa = self.fin(sfi, 'sfa')
        self.w('scalar', 'sfa', sfa)
        self.w('scalar', 'pnd', c - 1)
        ei = None
        if a:
            ei = a.activation(t.ebuf2[:, cp * 64:(cp + 1) * 64],
                              t.ebuf[:, cp * 64:(cp + 1) * 64], AF.Exp)
        sfb = self.fin(ei, 'sfb')
        self.w('vector', 'sfb', sfb)
        self.w('vector', 'ptm', c - 1)
        pni = None
        e3 = None
        if v:
            e3 = t.ebuf2[:, cp * 64:(cp + 1) * 64].rearrange("p (q h) -> p q h", q=NT)
            pni = v.tensor_reduce(t.s4[:, cp * 4:(cp + 1) * 4], e3, AX.X, ALU.add)
        self.vbar(pni)
        if v:
            pni = v.reciprocal(t.rs4[:, cp * 4:(cp + 1) * 4], t.s4[:, cp * 4:(cp + 1) * 4])
        self.vbar(pni)
        if v:
            rs3 = t.rs4[:, cp * 4:(cp + 1) * 4].rearrange("p (q o) -> p q o", o=1)
            e3b, rs3b = bcast(e3, rs3)
            pni = v.tensor_tensor(t.pinb[:, cp * 64:(cp + 1) * 64].rearrange("p (q h) -> p q h", q=NT),
                                  e3b, rs3b, ALU.mult)
        self.fin(pni, 'pnd')
        self.w('tensor', 'pnd', c + 1)
        self.w('tensor', 'ptv', c - 1)
        last = None
        if te:
            for q in range(NT):
                last = te.transpose(t.pitp[cp][:16, q * 128:(q + 1) * 128],
                                    t.pinb[:, cp * 64 + q * 16: cp * 64 + (q + 1) * 16],
                                    t.ident[:])
        self.fin(last, 'ptm')

    def a2_back(self, c):
        t = self.t
        cp = c % 2
        v, te, s = self.on('vector'), self.on('tensor'), self.on('sync')
        self.w('vector', 'ptm', c + 1)
        self.w('vector', 'pbm', NFD * (c - 1))
        self.w('vector', 'ptsA' if cp == 0 else 'ptsB', 16 * (c // 2))
        pv = None
        if v:
            v.tensor_copy(t.piTc[:16, cp * CH:(cp + 1) * CH], t.pitp[cp][:16, :])
            pv = v.tensor_reduce(t.Sacc[:16, c:c + 1],
                                 t.piTc[:16, cp * CH:(cp + 1) * CH], AX.X, ALU.add)
        ptv = self.fin(pv, 'ptv')
        self.w('sync', 'ptv', ptv)
        if s:
            s.dma_start(t.pit_h[:, c * CH:(c + 1) * CH],
                        t.piTc[:16, cp * CH:(cp + 1) * CH]) \
                .then_inc(self.sems['ptsA' if cp == 0 else 'ptsB'], 16)
        self.inc('ptsA' if cp == 0 else 'ptsB', 16)

        for f in range(NFD):
            idx = c * NFD + f
            bank = t.pibu[idx % 2]
            self.w('tensor', 'ptv', ptv)
            self.w('tensor', 'ust', idx - 1)
            last = None
            if te:
                last = te.matmul(bank[:], t.M2[:16, f * 128:(f + 1) * 128],
                                 t.piTc[:16, cp * CH:(cp + 1) * CH],
                                 start=True, stop=True)
            pb = self.fin(last, 'pbm')
            self.w('vector', 'pbm', pb)
            ui = None
            if v:
                ui = v.scalar_tensor_tensor(t.ujunk[:], t.wn2f[cp][:, f * CH:(f + 1) * CH],
                                            0.0, bank[:], ALU.bypass, ALU.mult,
                                            accum_out=t.Uacc[:, f * NCH + c: f * NCH + c + 1])
            self.fin(ui, 'ust')

    def a2(self):
        t = self.t
        for c in range(NCH):
            self.a2_front1a(c)
            if c >= 1:
                self.a2_back(c - 1)
            self.a2_front1b(c)
            self.a2_front2(c)
        self.a2_back(NCH - 1)

        v, g = self.on('vector'), self.on('gpsimd')
        self.w('vector', 'ust', NFD * NCH)
        sv = None
        if v:
            sv = v.tensor_reduce(t.Sfin[:16, :], t.Sacc[:16, :], AX.X, ALU.add)
        ssv = self.fin(sv, 'ssv')
        self.w('vector', 'ssv', ssv)
        ni = None
        if v:
            ni = v.nop()
        ssw = self.fin(ni, 'ssw')
        self.w('gpsimd', 'ssw', ssw)
        if g:
            g.dma_start(t.s_h, t.Sfin[:16, :]).then_inc(self.sems['dsc'], 16)
        d3 = self.inc('dsc', 16)
        self.w('gpsimd', 'dsc', d3)
        if g:
            g.dma_start(t.SBfull[:], bass.AP(t.s_h.tensor, 0, [[0, 128], [1, 16]])) \
                .then_inc(self.sems['dsc'], 16)
        d4 = self.inc('dsc', 16)
        self.w('vector', 'dsc', d4)
        ci = None
        if v:
            for f in range(NFD):
                v.tensor_copy(t.SB[0:64, f:f + 1], t.SBfull[0:64, 2 * f:2 * f + 1])
                ci = v.tensor_copy(t.SB[64:128, f:f + 1], t.SBfull[64:128, 2 * f + 1:2 * f + 2])
        self.vbar(ci)
        if v:
            ci = v.tensor_reduce(t.Ufin[:], t.Uacc[:].rearrange("p (f c) -> p f c", f=NFD), AX.X, ALU.add)
        self.vbar(ci)
        if v:
            ci = v.tensor_scalar_add(t.ct1[:], t.SB[:], 1e-8)
        self.vbar(ci)
        if v:
            ci = v.reciprocal(t.ct2[:], t.ct1[:])
        self.vbar(ci)
        if v:
            ci = v.tensor_tensor(t.ct3[:], t.Ufin[:], t.ct2[:], ALU.mult)
        self.vbar(ci)
        if v:
            ci = v.tensor_scalar_add(t.ct4[:], t.ct3[:], 1.0)
        self.vbar(ci)
        if v:
            ci = v.reciprocal(t.ct5[:], t.ct4[:])
        self.vbar(ci)
        if v:
            ci = v.tensor_scalar_mul(t.Cneg[:], t.ct5[:], -1.0)
        cd = self.fin(ci, 'cdn')
        self.w('vector', 'cdn', cd)

    def a3(self):
        t = self.t
        s = self.on('sync')
        self.w('sync', 'qkm', 2 * NFD * NCH)
        self.w('sync', 'ust', NFD * NCH)
        if s:
            s.dma_start(t.wqT[:], t.woT_h).then_inc(self.sems['wol'], 16)
        wol = self.inc('wol', 16)

        for c in range(NCH):
            s = self.on('sync')
            self.w('sync', 'pb3', NFD * (c - 1))
            if c < 2:
                self.w('sync', 'pbm', NFD * NCH)
            self.w('sync', 'ptsA' if c % 2 == 0 else 'ptsB', 16 * (c // 2 + 1))
            ptlsem = 'ptlA' if c % 2 == 0 else 'ptlB'
            if s:
                s.dma_start(t.piTc[:16, (c % 2) * CH:(c % 2 + 1) * CH],
                            t.pit_h[:, c * CH:(c + 1) * CH]).then_inc(self.sems[ptlsem], 16)
            ptl = self.inc(ptlsem, 16)

            te, v = self.on('tensor'), self.on('vector')
            for f in range(NFD):
                idx = c * NFD + f
                bank = t.pib[idx % 2]
                self.w('tensor', ptlsem, ptl)
                if idx == 0:
                    self.w('tensor', 'ust', NFD * NCH)
                self.w('tensor', 'osd', idx - 1)
                last = None
                if te:
                    last = te.matmul(bank[:], t.M2[:16, f * 128:(f + 1) * 128],
                                     t.piTc[:16, (c % 2) * CH:(c % 2 + 1) * CH],
                                     start=True, stop=True)
                pb3 = self.fin(last, 'pb3')
                self.w('vector', 'pb3', pb3)
                if c > 0 and f == 0:
                    self.w('vector', 'aom', 2 * NT * c)
                oi = None
                if v:
                    oi = v.scalar_tensor_tensor(t.oTbuf[:, f * CH:(f + 1) * CH],
                                                t.wT[:, f * NTOK + c * CH: f * NTOK + (c + 1) * CH],
                                                t.Cneg[:, f:f + 1], bank[:],
                                                ALU.mult, ALU.mult)
                self.fin(oi, 'osd')

            s = self.on('sync')
            if c == 0:
                self.w('sync', 'lnf', NTILES)
            for tt in range(NT):
                gidx = c * NT + tt
                self.w('sync', 'rev', 2 * (gidx - 1))
                if s:
                    s.dma_start(t.xbuf[tt % 2][:], t.x[gidx * 128:(gidx + 1) * 128, :]) \
                        .then_inc(self.sems['xl2A' if gidx % 2 == 0 else 'xl2B'], 16)
                self.inc('xl2A' if gidx % 2 == 0 else 'xl2B', 16)

            te = self.on('tensor')
            self.w('tensor', 'osd', self.cnt['osd'])
            if c == 0:
                self.w('tensor', 'wol', wol)
            for tt in range(NT):
                for nh in range(2):
                    aidx = (c * NT + tt) * 2 + nh
                    bank = t.aop[aidx % 3]
                    self.w('tensor', 'rev', aidx - 2)
                    last = None
                    if te:
                        for f in range(NFD):
                            last = te.matmul(bank[:],
                                             t.oTbuf[:, f * CH + tt * 128: f * CH + tt * 128 + 128],
                                             t.wqT[:, f * 1024 + nh * CH: f * 1024 + (nh + 1) * CH],
                                             start=(f == 0), stop=False)
                        last = te.matmul(bank[:], t.onesrow[:], t.b_bf[0:1, nh * CH:(nh + 1) * CH],
                                         start=False, stop=True)
                    self.fin(last, 'aom')

            v, a = self.on('vector'), self.on('scalar')
            for tt in range(NT):
                gidx = c * NT + tt
                self.w('vector', 'aom', (c * NT + tt) * 2 + 2)
                self.w('vector', 'xl2A' if gidx % 2 == 0 else 'xl2B', 16 * (gidx // 2 + 1))
                self.w('vector', 'rstA' if gidx % 2 == 0 else 'rstB', 16 * (gidx // 2))
                for nh in range(2):
                    aidx = (c * NT + tt) * 2 + nh
                    ri = None
                    if v:
                        ri = v.tensor_tensor(t.Rbuf[:, (tt % 2) * D + nh * CH:(tt % 2) * D + (nh + 1) * CH],
                                             t.aop[aidx % 3][:],
                                             t.xbuf[tt % 2][:, nh * CH:(nh + 1) * CH], ALU.add)
                    self.fin(ri, 'rev')
                rv = self.cnt['rev']
                self.w('scalar', 'rev', rv)
                if a:
                    a.dma_start(t.r_h[gidx * 128:(gidx + 1) * 128, :],
                                t.Rbuf[:, (tt % 2) * D:(tt % 2 + 1) * D]) \
                        .then_inc(self.sems['rstA' if gidx % 2 == 0 else 'rstB'], 16)
                self.inc('rstA' if gidx % 2 == 0 else 'rstB', 16)

    def f(self):
        t = self.t
        s, te, a, v = self.on('sync'), self.on('tensor'), self.on('scalar'), self.on('vector')

        self.w('sync', 'osd', NFD * NCH)
        self.w('sync', 'pb3', NFD * NCH)
        self.w('sync', 'rstA', 16 * (NTILES // 2))
        self.w('sync', 'rstB', 16 * (NTILES // 2))
        for pw in ['pw2A', 'pw2B', 'pw1A', 'pw1B', 'pw0A', 'pw0B']:
            self.w('sync', pw, self.cnt[pw])
        if s:
            s.dma_start(t.w2res[:], t.w2s_h).then_inc(self.sems['w2l'], 16)
        w2l = self.inc('w2l', 16)

        for c in range(NCH):
            self.w('sync', 'h1m', F1T * c)
            if s:
                src = t.xnt_h[:, :, c * CH:(c + 1) * CH].rearrange("f p tok -> p f tok")
                if c == 0:
                    s.wait_ge(self.sems['xns'], 32 * NCH)
                s.dma_start(t.xnl[:].rearrange("p (f tok) -> p f tok", f=NFD), src) \
                    .then_inc(self.sems['xnl'], 16)
            xnl = self.inc('xnl', 16)

            for fi in range(FPC):
                fg = c * FPC + fi
                par = fg % 2
                par3 = fg % 3
                fsem = ['wflA', 'wflB', 'wflC'][par3]
                self.w('sync', 'wcv', fg - 2)
                if fi < 2:
                    self.w('sync', 'wcv', (c - 1) * FPC + fi + 7)
                elif fi < 8:
                    self.w('sync', 'wcv', fg - 1)
                if s:
                    if fi < 8:
                        s.dma_start(t.u8p[par3][:], t.k0ts[fi]).then_inc(self.sems[fsem], 16)
                        s.dma_start(t.u81[par][:], t.k0bs[fi]).then_inc(self.sems[fsem], 16)
                    else:
                        s.dma_start(t.u8p[par3][:], t.k1s[fi - 8]).then_inc(self.sems[fsem], 16)
                self.inc(fsem, 32 if fi < 8 else 16)
                vd = self.on('vector')
                self.w('vector', fsem, self.cnt[fsem])
                if fg >= 2:
                    cp2, fip = (fg - 2) // FPC, (fg - 2) % FPC
                    if fip < 8:
                        self.w('vector', 'h1m', cp2 * F1T + 4 * (fip + 1))
                    else:
                        self.w('vector', 'h2m', cp2 * F2T + (fip - 8) + 1)
                ci = None
                if vd:
                    if fi < 8:
                        ci = vd.tensor_tensor(t.wbuf[par][:], t.u8p[par3][:], t.u81[par][:],
                                              ALU.subtract)
                    else:
                        ci = vd.tensor_scalar(t.wbuf[par][:], t.u8p[par3][:], 2.0, -1.0,
                                              ALU.mult, ALU.add)
                self.fin(ci, 'wcv')

            self.w('tensor', 'xnl', xnl)
            if c == 0:
                self.w('tensor', 'w2l', w2l)
            for m1 in range(F1T):
                fg = c * FPC + m1 // 4
                self.w('tensor', 'wcv', fg + 1)
                self.w('tensor', 'h1e', c * F1T + m1 - 3)
                bank = t.hb[m1 % 4]
                last = None
                if te:
                    for k in range(NFD):
                        last = te.matmul(bank[:],
                                         t.wbuf[fg % 2][:, (m1 % 4) * 1024 + k * 128:(m1 % 4) * 1024 + k * 128 + 128],
                                         t.xnl[:, k * CH:(k + 1) * CH],
                                         start=(k == 0), stop=(k == NFD - 1))
                h1m = self.fin(last, 'h1m')
                self.w('scalar', 'h1m', h1m)
                if c > 0 and m1 == 0:
                    self.w('scalar', 'h2m', F2T * c)
                ei = None
                if a:
                    ei = a.activation(t.h1t[:, m1 * CH:(m1 + 1) * CH], bank[:], AF.Relu,
                                      scale=t.scalsB[:, 0:1])
                self.fin(ei, 'h1e')

            self.w('tensor', 'h1e', F1T * (c + 1))
            for m2 in range(F2T):
                fg = c * FPC + 8 + m2
                self.w('tensor', 'wcv', fg + 1)
                self.w('tensor', 'h2e', c * F2T + m2 - 3)
                bank = t.hb[m2 % 4]
                last = None
                if te:
                    for k in range(F1T):
                        last = te.matmul(bank[:], t.wbuf[fg % 2][:, k * 128:(k + 1) * 128],
                                         t.h1t[:, k * CH:(k + 1) * CH],
                                         start=(k == 0), stop=(k == F1T - 1))
                h2m = self.fin(last, 'h2m')
                self.w('scalar', 'h2m', h2m)
                if c > 0 and m2 == 0:
                    self.w('scalar', 'ym', 2 * NT * c)
                ei = None
                if a:
                    ei = a.activation(t.h2t[:, m2 * CH:(m2 + 1) * CH], bank[:], AF.Relu,
                                      scale=t.scalsB[:, 1:2])
                self.fin(ei, 'h2e')

            for tt in range(NT):
                gidx = c * NT + tt
                self.w('sync', 'yev', 2 * (gidx - 1))
                if s:
                    s.dma_start(t.xbuf[tt % 2][:], t.r_h[gidx * 128:(gidx + 1) * 128, :]) \
                        .then_inc(self.sems['rldA' if gidx % 2 == 0 else 'rldB'], 16)
                self.inc('rldA' if gidx % 2 == 0 else 'rldB', 16)

            self.w('tensor', 'h2e', F2T * (c + 1))
            for tt in range(NT):
                for nh in range(2):
                    yidx = (c * NT + tt) * 2 + nh
                    bank = t.yb[yidx % 4]
                    self.w('tensor', 'yev', yidx - 3)
                    last = None
                    if te:
                        for k in range(F2T):
                            last = te.matmul(bank[:],
                                             t.h2t[:, k * CH + tt * 128: k * CH + tt * 128 + 128],
                                             t.w2res[:, k * 1024 + nh * CH: k * 1024 + (nh + 1) * CH],
                                             start=(k == 0), stop=(k == F2T - 1))
                    self.fin(last, 'ym')

                gidx = c * NT + tt
                self.w('vector', 'ym', (c * NT + tt) * 2 + 2)
                self.w('vector', 'rldA' if gidx % 2 == 0 else 'rldB', 16 * (gidx // 2 + 1))
                self.w('vector', 'ostA' if gidx % 2 == 0 else 'ostB', 16 * (gidx // 2))
                if c == 0 and tt == 0:
                    self.w('vector', 'rstA', 16 * (NTILES // 2))
                    self.w('vector', 'rstB', 16 * (NTILES // 2))
                for nh in range(2):
                    yidx = (c * NT + tt) * 2 + nh
                    yi = None
                    if v:
                        yi = v.scalar_tensor_tensor(t.Rbuf[:, (tt % 2) * D + nh * CH:(tt % 2) * D + (nh + 1) * CH],
                                                    t.yb[yidx % 4][:], t.scalsB[:, 2:3],
                                                    t.xbuf[tt % 2][:, nh * CH:(nh + 1) * CH],
                                                    ALU.mult, ALU.add)
                    self.fin(yi, 'yev')
                yv = self.cnt['yev']
                self.w('scalar', 'yev', yv)
                if a:
                    a.dma_start(t.out[gidx * 128:(gidx + 1) * 128, :],
                                t.Rbuf[:, (tt % 2) * D:(tt % 2 + 1) * D]) \
                        .then_inc(self.sems['ostA' if gidx % 2 == 0 else 'ostB'], 16)
                self.inc('ostA' if gidx % 2 == 0 else 'ostB', 16)

    def drain_dma_sems(self):
        for eng, sems in [('sync', ['ild', 'xld0', 'xld1', 'xld2', 'xld3', 'xl2A', 'xl2B',
                                    'ptlA', 'ptlB', 'wol', 'w2l',
                                    'wflA', 'wflB', 'wflC', 'xnl', 'rldA', 'rldB', 'ptsA', 'ptsB']),
                          ('scalar', ['wth', 'xns', 'rstA', 'rstB', 'ostA', 'ostB']),
                          ('gpsimd', ['dsc', 'pldA', 'pldB', 'pw2A', 'pw2B',
                                      'pw1A', 'pw1B', 'pw0A', 'pw0B', 'dbg'])]:
            for sm in sems:
                if self.cnt[sm] > 0:
                    self.w(eng, sm, self.cnt[sm])

    def tail(self):
        t = self.t
        g = self.on('gpsimd')
        if self.phases >= 4 and not self.debug:
            self.w('scalar', 'ostA', 16 * (NTILES // 2))
            self.w('scalar', 'ostB', 16 * (NTILES // 2))
            self.drain_dma_sems()
            return
        if self.debug:
            self.w('gpsimd', 'wve', 2 * NFD * NCH)
            self.w('gpsimd', 'nsq', NFD * NCH)
            self.w('gpsimd', 'rnd', 1)
            if self.phases >= 2:
                self.w('gpsimd', 'ptsA', 16 * (NCH // 2))
                self.w('gpsimd', 'ptsB', 16 * (NCH // 2))
                self.w('gpsimd', 'cdn', 1)
            if self.phases >= 3:
                self.w('gpsimd', 'rstA', 16 * (NTILES // 2))
                self.w('gpsimd', 'rstB', 16 * (NTILES // 2))
            if self.phases >= 4:
                self.w('gpsimd', 'ostA', 16 * (NTILES // 2))
                self.w('gpsimd', 'ostB', 16 * (NTILES // 2))
            nd = 0
            if self.phases < 4:
                if g:
                    g.dma_start(t.dwT, t.wT[:, 0:2048]).then_inc(self.sems['dbg'], 16)
                nd += 16
            if g:
                g.dma_start(t.dnsq, t.nsqacc[:]).then_inc(self.sems['dbg'], 16)
                g.dma_start(t.drn2, t.rn2[:]).then_inc(self.sems['dbg'], 16)
                g.dma_start(t.dxn, t.xnt_h[0, :, 0:1024]).then_inc(self.sems['dbg'], 16)
            nd += 3 * 16
            if self.phases >= 2:
                if g:
                    g.dma_start(t.dpit, t.pit_h[:, 0:1024]).then_inc(self.sems['dbg'], 16)
                    g.dma_start(t.dC, t.Cneg[:]).then_inc(self.sems['dbg'], 16)
                nd += 2 * 16
            if self.phases >= 3:
                if g:
                    g.dma_start(t.dR[0:128, :], t.r_h[0:128, :]).then_inc(self.sems['dbg'], 16)
                    g.dma_start(t.dR[128:256, :], t.r_h[4096:4224, :]).then_inc(self.sems['dbg'], 16)
                nd += 2 * 16
            if self.phases >= 4:
                if g:
                    g.dma_start(t.dh1, t.h1t[:]).then_inc(self.sems['dbg'], 16)
                    g.dma_start(t.dh2, t.h2t[:]).then_inc(self.sems['dbg'], 16)
                    g.dma_start(t.dw0, t.w0s_h[0]).then_inc(self.sems['dbg'], 16)
                    g.dma_start(t.dw1, t.w1s_h[0]).then_inc(self.sems['dbg'], 16)
                    g.dma_start(t.dw2, t.w2s_h[:, 0:4096]).then_inc(self.sems['dbg'], 16)
                nd += 5 * 16
            self.inc('dbg', nd)
            self.w('gpsimd', 'dbg', nd)
        else:
            if self.phases >= 3:
                self.w('scalar', 'rstA', 16 * (NTILES // 2))
                self.w('scalar', 'rstB', 16 * (NTILES // 2))
            else:
                self.w('scalar', 'wve', 2 * NFD * NCH)
        self.drain_dma_sems()


class T:
    pass


def build(phases=4, debug=False):
    nc = bass.Bass("TRN2", target_bir_lowering=False, detect_race_conditions=False)
    t = T()

    t.x = nc.dram_tensor("x", [NTOK, D], F32, kind="ExternalInput").ap()
    t.wqkv = nc.dram_tensor("wqkv", [D, D], F32, kind="ExternalInput").ap()
    t.temp = nc.dram_tensor("temp", [16, 1], F32, kind="ExternalInput").ap()
    t.wout = nc.dram_tensor("wout", [D, D], F32, kind="ExternalInput").ap()
    t.bout = nc.dram_tensor("bout", [1, D], F32, kind="ExternalInput").ap()
    t.k1s = nc.dram_tensor("k1s", [32, 128, 4096], U8, kind="ExternalInput").ap()
    t.k0ts = nc.dram_tensor("k0ts", [8, 128, 4096], U8, kind="ExternalInput").ap()
    t.k0bs = nc.dram_tensor("k0bs", [8, 128, 4096], U8, kind="ExternalInput").ap()
    t.k2 = nc.dram_tensor("k2", [4096, 1024], U8, kind="ExternalInput").ap()
    t.s0 = nc.dram_tensor("s0", [1, 1], F32, kind="ExternalInput").ap()
    t.s1 = nc.dram_tensor("s1", [1, 1], F32, kind="ExternalInput").ap()
    t.s2 = nc.dram_tensor("s2", [1, 1], F32, kind="ExternalInput").ap()
    t.out = nc.dram_tensor("out", [NTOK, D], F32, kind="ExternalOutput").ap()

    t.xnt_h = nc.dram_tensor("xnt_h", [NFD, 128, NTOK], BF16, kind="Internal").ap()
    t.r_h = nc.dram_tensor("r_h", [NTOK, D], F32, kind="Internal").ap()
    t.pit_h = nc.dram_tensor("pit_h", [16, NTOK], BF16, kind="Internal").ap()
    t.w2s_h = nc.dram_tensor("w2s_h", [128, F2T * 1024], BF16, kind="Internal").ap()
    t.der_h = nc.dram_tensor("der_h", [1, 8], F32, kind="Internal").ap()
    t.s_h = nc.dram_tensor("s_h", [16, 1], F32, kind="Internal").ap()
    t.woT_h = nc.dram_tensor("woT_h", [128, NFD * 1024], BF16, kind="Internal").ap()

    if debug:
        t.dxn = nc.dram_tensor("dxn", [128, 1024], BF16, kind="ExternalOutput").ap()
        t.dwT = nc.dram_tensor("dwT", [128, 2048], BF16, kind="ExternalOutput").ap()
        t.dnsq = nc.dram_tensor("dnsq", [128, 128], F32, kind="ExternalOutput").ap()
        t.drn2 = nc.dram_tensor("drn2", [128, 8], F32, kind="ExternalOutput").ap()
        if phases >= 2:
            t.dpit = nc.dram_tensor("dpit", [16, 1024], BF16, kind="ExternalOutput").ap()
            t.dC = nc.dram_tensor("dC", [128, 8], F32, kind="ExternalOutput").ap()
        if phases >= 3:
            t.dR = nc.dram_tensor("dR", [256, D], F32, kind="ExternalOutput").ap()
        if phases >= 4:
            t.dh1 = nc.dram_tensor("dh1", [128, F1T * CH], BF16, kind="ExternalOutput").ap()
            t.dh2 = nc.dram_tensor("dh2", [128, F2T * CH], BF16, kind="ExternalOutput").ap()
            t.dw0 = nc.dram_tensor("dw0", [128, 4096], BF16, kind="ExternalOutput").ap()
            t.dw1 = nc.dram_tensor("dw1", [128, 4096], BF16, kind="ExternalOutput").ap()
            t.dw2 = nc.dram_tensor("dw2", [128, 4096], BF16, kind="ExternalOutput").ap()

    sb = nc.alloc_sbuf_tensor
    t.zv = sb("zv", [128, 1], F32)
    t.ov = sb("ov", [128, 1], F32)
    t.epsv = sb("epsv", [128, 1], F32)
    t.onesrow = sb("onesrow", [1, 128], BF16)
    t.ident = sb("ident", [128, 128], BF16)
    t.E = sb("E", [128, 128], BF16)
    t.M2 = sb("M2", [16, NFD * 128], BF16)
    t.tempB = sb("tempB", [128, 16], F32)
    t.scalsB = sb("scalsB", [128, 8], F32)
    t.svec = sb("svec", [1, 4], F32)
    t.der = sb("der", [1, 8], F32)
    t.rn2 = sb("rn2", [128, NFD], F32)
    t.rn2a = sb("rn2a", [128, NFD], F32)
    t.rn2b = sb("rn2b", [128, NFD], F32)
    t.SB = sb("SBt", [128, NFD], F32)
    t.SBfull = sb("SBfull", [128, 16], F32)
    t.Cneg = sb("Cneg", [128, NFD], F32)
    t.nsqacc = sb("nsqacc", [128, 128], F32)
    t.Uacc = sb("Uacc", [128, 128], F32)
    t.Ufin = sb("Ufin", [128, NFD], F32)
    t.Sacc = sb("Sacc", [16, 16], F32)
    t.Sfin = sb("Sfin", [16, 1], F32)
    t.ct1 = sb("ct1", [128, NFD], F32)
    t.ct2 = sb("ct2", [128, NFD], F32)
    t.ct3 = sb("ct3", [128, NFD], F32)
    t.ct4 = sb("ct4", [128, NFD], F32)
    t.ct5 = sb("ct5", [128, NFD], F32)
    t.b_bf = sb("b_bf", [1, D], BF16)
    for nm in ["lnsum", "lnm", "lnmsq", "lnt1", "ssqs", "lnvar", "lnsd", "lnrstd"]:
        setattr(t, nm, sb(nm, [128, 2 * NT], F32))
    t.ltbuf = sb("ltbuf", [128, 128], F32)
    t.nmx4 = sb("nmx4", [128, 8], F32)
    t.ebuf = sb("ebuf", [128, 128], F32)
    t.ebuf2 = sb("ebuf2", [128, 128], F32)
    t.s4 = sb("s4", [128, 8], F32)
    t.rs4 = sb("rs4", [128, 8], F32)
    t.pinb = sb("pinb", [128, 128], BF16)
    t.lgT = sb("lgT", [16, 2 * CH], BF16)
    t.Ufin2 = sb("Ufin2", [128, NFD], F32)
    t.junk = sb("junk", [128, D], BF16)
    t.junkb = sb("junkb", [128, CH], BF16)
    t.wn2 = sb("wn2", [128, 2 * CH], BF16)
    xr = sb("xr", [128, 16384], U8)
    xrbase = nc.lookup_mloc(xr).addr
    t.xbuf4 = [nc.alloc_sbuf_tensor_at(f"xb4_{i}", [128, D], F32, offset=xrbase + 4096 * i)
               for i in range(4)]
    t.Rbuf = nc.alloc_sbuf_tensor_at("Rbuf", [128, 2 * D], F32, offset=xrbase + 8192)
    t.xbuf = t.xbuf4[:2]
    t.piTc = sb("piTc", [16, 2 * CH], BF16)

    ARENA = 178176
    arena = nc.alloc_sbuf_tensor("arena", [128, ARENA], U8)
    abase = nc.lookup_mloc(arena).addr

    def at(name, shape, dtype, off):
        return nc.alloc_sbuf_tensor_at(name, shape, dtype, offset=abase + off)

    t.wT = at("wT", [128, NFD * NTOK], BF16, 0)
    t.wqT = at("wqT", [128, NFD * 1024], BF16, 131072)
    t.xnTcH = [at("xnTcH0", [128, NFD * 256], BF16, 147456),
               at("xnTcH1", [128, NFD * 256], BF16, 151552)]
    t.oTbuf = at("oTbuf", [128, NFD * CH], BF16, 147456)
    t.xnbuf = at("xnbuf", [128, 2 * D], BF16, 155648)
    t.wq32 = at("wq32", [128, NFD * 1024], F32, 0)
    t.wo32 = at("wo32", [128, NFD * 1024], F32, 32768)
    t.wqb = at("wqb", [128, NFD * 1024], BF16, 65536)
    t.wob = at("wob", [128, NFD * 1024], BF16, 81920)
    t.b32 = at("b32", [1, D], F32, 98304)
    t.ujunk = at("ujunk", [128, CH], BF16, 139264)
    t.wn2f = [at("wn2fA", [128, NFD * CH], BF16, 147456),
              at("wn2fB", [128, NFD * CH], BF16, 131072)]
    t.pA2 = [at("pA2a", [128, 4096], U8, 159744), at("pA2b", [128, 4096], U8, 163840)]
    t.pD2 = [at("pD2a", [128, 1024], BF16, 167936), at("pD2b", [128, 1024], BF16, 169984)]
    t.u8p = [at("u80a", [128, 4096], U8, 159744), at("u80b", [128, 4096], U8, 163840),
             at("u80c", [128, 4096], U8, 155648)]
    t.u81 = [at("u81a", [128, 4096], U8, 167936), at("u81b", [128, 4096], U8, 172032)]
    t.w2res = at("w2res", [128, F2T * 1024], BF16, 0)
    t.wbuf = [at("wbufA", [128, 4096], BF16, 65536), at("wbufB", [128, 4096], BF16, 73728)]
    t.h1t = at("h1t", [128, F1T * CH], BF16, 81920)
    t.h2t = at("h2t", [128, F2T * CH], BF16, 114688)
    t.xnl = at("xnl", [128, NFD * CH], BF16, 147456)

    psnap = nc.psum_base
    t.ptr = [nc.alloc_psum_tensor("ptr0", [128, 1024], BF16),
             nc.alloc_psum_tensor("ptr1", [128, 1024], BF16)]
    t.qkvp = [nc.alloc_psum_tensor(f"qkvp{i}", [128, CH], F32) for i in range(6)]
    nc.psum_base = psnap
    t.lgp = [nc.alloc_psum_tensor(f"lgp{i}", [16, CH], F32) for i in range(2)]
    t.lgnp = [nc.alloc_psum_tensor(f"lgnp{i}", [128, 64], BF16) for i in range(2)]
    t.pitp = [nc.alloc_psum_tensor(f"pitp{i}", [16, CH], BF16) for i in range(2)]
    t.pibu = [nc.alloc_psum_tensor(f"pibu{i}", [128, CH], F32) for i in range(2)]
    nc.psum_base = psnap
    t.pib = [nc.alloc_psum_tensor(f"pib{i}", [128, CH], F32) for i in range(2)]
    t.aop = [nc.alloc_psum_tensor(f"aop{i}", [128, CH], F32) for i in range(3)]
    nc.psum_base = psnap
    t.hb = [nc.alloc_psum_tensor(f"hb{i}", [128, CH], F32) for i in range(4)]
    t.yb = [nc.alloc_psum_tensor(f"yb{i}", [128, CH], F32) for i in range(4)]

    nc.const_aps.register(F32, 0.0, t.zv[:])
    nc.const_aps.register(F32, 1.0, t.ov[:])

    semnames = ['vbr', 'ini', 'inv', 'ild', 'dsc', 'wtt', 'wts', 'wth', 'wtb', 'm2t', 'm2v',
                'pldA', 'pldB', 'pw2A', 'pw2B', 'pw1A', 'pw1B', 'pw0A', 'pw0B',
                'xld0', 'xld1', 'xld2', 'xld3', 'sqd', 'lnA', 'lnB', 'lnr', 'lnf', 'trp', 'xev', 'xns',
                'qkm', 'wve', 'nsq', 'rnd', 'rns', 'epv', 'lsg',
                'wnv', 'emm', 'lgm', 'lgv', 'lgt', 'pnv', 'sfa', 'sfb', 'pnd',
                'ptm', 'ptv', 'ptsA', 'ptsB', 'pbm', 'ust', 'ssv', 'ssw', 'cdn',
                'wol', 'ptlA', 'ptlB', 'pb3', 'osd', 'aom', 'xl2A', 'xl2B', 'rev', 'rstA', 'rstB',
                'w2l', 'wflA', 'wflB', 'wflC', 'wcv', 'xnl', 'h1m', 'h1e', 'h2m', 'h2e', 'ym', 'yev',
                'rldA', 'rldB', 'ostA', 'ostB', 'dbg']
    stk = ExitStack()
    sems = {n: stk.enter_context(nc.semaphore(n)) for n in semnames}

    builder = B(nc, sems, t, phases, debug)
    with nc.Block() as block:
        @block.sync
        def _(sync):
            builder.run('sync', sync)

        @block.tensor
        def _(tensor):
            builder.run('tensor', tensor)

        @block.vector
        def _(vector):
            builder.run('vector', vector)

        @block.scalar
        def _(scalar):
            builder.run('scalar', scalar)

        @block.gpsimd
        def _(gpsimd):
            builder.run('gpsimd', gpsimd)

    stk.close()
    return nc


def kernel(L0_attn_x, L0_attn_W_qkv, L0_attn_temp, L0_attn_W_out, L0_attn_b_out,
           L0_ffn_k0, L0_ffn_k1, L0_ffn_k2, L0_ffn_s0, L0_ffn_s1, L0_ffn_s2):
    x = np.ascontiguousarray(np.asarray(L0_attn_x, dtype=np.float32))
    shared = {
        "wqkv": np.ascontiguousarray(np.asarray(L0_attn_W_qkv, dtype=np.float32)),
        "temp": np.ascontiguousarray(np.asarray(L0_attn_temp, dtype=np.float32).reshape(16, 1)),
        "wout": np.ascontiguousarray(np.asarray(L0_attn_W_out, dtype=np.float32)),
        "bout": np.ascontiguousarray(np.asarray(L0_attn_b_out, dtype=np.float32).reshape(1, D)),
        "k1s": np.ascontiguousarray(
            np.asarray(L0_ffn_k1).astype(np.uint8).reshape(32, 128, 32, 128)
            .transpose(2, 1, 0, 3).reshape(32, 128, 4096)),
        "k0ts": np.ascontiguousarray(
            np.asarray(L0_ffn_k0).astype(np.uint8)[:1024].reshape(8, 128, 8, 4, 128)
            .transpose(2, 1, 3, 0, 4).reshape(8, 128, 4096)),
        "k0bs": np.ascontiguousarray(
            np.asarray(L0_ffn_k0).astype(np.uint8)[1024:].reshape(8, 128, 8, 4, 128)
            .transpose(2, 1, 3, 0, 4).reshape(8, 128, 4096)),
        "k2": np.ascontiguousarray(np.asarray(L0_ffn_k2).astype(np.uint8)),
        "s0": np.ascontiguousarray(np.asarray(L0_ffn_s0, dtype=np.float32).reshape(1, 1)),
        "s1": np.ascontiguousarray(np.asarray(L0_ffn_s1, dtype=np.float32).reshape(1, 1)),
        "s2": np.ascontiguousarray(np.asarray(L0_ffn_s2, dtype=np.float32).reshape(1, 1)),
    }
    nc = build()
    in_maps = [dict(shared, x=np.ascontiguousarray(x[b])) for b in range(8)]
    res = run_bass_kernel_spmd(nc, in_maps, core_ids=list(range(8)))
    return np.stack([r["out"] for r in res.results], axis=0)


# version 23 (final)
# speedup vs baseline: 1.0152x; 1.0022x over previous
import numpy as np
from collections import defaultdict
from contextlib import ExitStack

import concourse.bass as bass
import concourse.mybir as mybir
from concourse.bass_utils import run_bass_kernel_spmd

F32 = mybir.dt.float32
BF16 = mybir.dt.bfloat16
U8 = mybir.dt.uint8
AF = mybir.ActivationFunctionType
ALU = mybir.AluOpType
AX = mybir.AxisListType
bcast = bass.broadcast_tensor_aps

NTOK = 8192
D = 1024
NCH = 16
CH = 512
NT = 4
NFD = 8
F1T = 32
F2T = 32
NTILES = NCH * NT
FPC = 8 + F2T


class B:
    def __init__(self, nc, sems, t, phases, debug):
        self.nc, self.sems, self.t = nc, sems, t
        self.phases, self.debug = phases, debug

    def run(self, eng, handle):
        self.eng, self.h = eng, handle
        self.cnt = defaultdict(int)
        self.emit()

    def on(self, eng):
        return self.h if self.eng == eng else None

    def w(self, eng, sem, val):
        if self.eng == eng and val > 0:
            self.h.wait_ge(self.sems[sem], val)

    def inc(self, sem, n=1):
        self.cnt[sem] += n
        return self.cnt[sem]

    def fin(self, inst, sem, n=1):
        if inst is not None:
            inst.then_inc(self.sems[sem], n)
        return self.inc(sem, n)

    def vbar(self, inst):
        val = self.fin(inst, 'vbr')
        self.w('vector', 'vbr', val)

    def emit(self):
        self.init()
        self.prep()
        self.a1()
        if self.phases >= 2:
            self.a2()
        if self.phases >= 3:
            self.a3()
        if self.phases >= 4:
            self.f()
        self.tail()

    def trans_1024(self, src, dst, tag_t, tag_e):
        t = self.t
        te, a = self.on('tensor'), self.on('scalar')
        for arow in range(8):
            bank = t.ptr[arow % 2]
            self.w('tensor', tag_e, self.cnt[tag_e] - 1)
            last = None
            if te:
                for jb in range(8):
                    last = te.transpose(bank[:, jb * 128:(jb + 1) * 128],
                                        src[:, arow * 1024 + jb * 128: arow * 1024 + jb * 128 + 128],
                                        t.ident[:])
            trd = self.fin(last, tag_t)
            self.w('scalar', tag_t, trd)
            ei = None
            if a:
                outap = dst[:].rearrange("p (j i) -> p j i", j=8)[:, :, arow * 128:arow * 128 + 128]
                ei = a.copy(outap, bank[:].rearrange("p (j i) -> p j i", j=8))
            self.fin(ei, tag_e)

    def init(self):
        t = self.t
        g, s, v = self.on('gpsimd'), self.on('sync'), self.on('vector')
        te, a = self.on('tensor'), self.on('scalar')

        last = None
        if g:
            g.memset(t.zv[:], 0.0)
            g.memset(t.ov[:], 1.0)
            g.memset(t.epsv[:], 1e-5)
            g.memset(t.onesrow[:], 1.0)
            g.memset(t.ident[:], 0.0)
            g.affine_select(out=t.ident[:], in_=t.ident[:], compare_op=ALU.not_equal,
                            fill=1.0, base=0, pattern=[[-1, 128]], channel_multiplier=1)
            g.memset(t.E[:], 0.0)
            for f in range(NFD):
                g.memset(t.E[0:64, f * 16 + 2 * f: f * 16 + 2 * f + 1], 1.0)
                last = g.memset(t.E[64:128, f * 16 + 2 * f + 1: f * 16 + 2 * f + 2], 1.0)
        ini = self.fin(last, 'ini')

        if s:
            s.dma_start(t.wq32[:], t.wqkv.rearrange("(a p) f -> p a f", p=128)).then_inc(self.sems['ild'], 16)
            s.dma_start(t.wo32[:], t.wout.rearrange("(a p) f -> p a f", p=128)).then_inc(self.sems['ild'], 16)
            s.dma_start(t.b32[:], t.bout).then_inc(self.sems['ild'], 16)
            for j, st in enumerate([t.s0, t.s1, t.s2]):
                s.dma_start(t.svec[:, j:j + 1], st).then_inc(self.sems['ild'], 16)
            s.dma_start(t.tempB[:], bass.AP(t.temp.tensor, 0, [[0, 128], [1, 16]])).then_inc(self.sems['ild'], 16)
        ild = self.inc('ild', 16 * 7)

        self.w('vector', 'ild', ild)
        last = None
        if v:
            v.memset(t.der[:], 0.0)
            v.tensor_copy(t.wqb[:], t.wq32[:])
            v.tensor_copy(t.wob[:], t.wo32[:])
            v.tensor_copy(t.b_bf[:], t.b32[:])
            v.tensor_tensor(t.der[:, 0:1], t.svec[:, 0:1], t.svec[:, 0:1], ALU.add)
            v.tensor_tensor(t.der[:, 1:2], t.svec[:, 1:2], t.svec[:, 1:2], ALU.add)
            last = v.tensor_tensor(t.der[:, 2:3], t.svec[:, 2:3], t.svec[:, 2:3], ALU.add)
        inv = self.fin(last, 'inv')

        self.w('gpsimd', 'inv', inv)
        if g:
            g.dma_start(t.der_h, t.der[:]).then_inc(self.sems['dsc'], 16)
        d1 = self.inc('dsc', 16)
        self.w('gpsimd', 'dsc', d1)
        if g:
            g.dma_start(t.scalsB[:], bass.AP(t.der_h.tensor, 0, [[0, 128], [1, 8]])).then_inc(self.sems['dsc'], 16)
        self.inc('dsc', 16)

        self.w('tensor', 'inv', inv)
        self.w('tensor', 'ini', ini)
        self.trans_1024(t.wob, t.wqT, 'wtt', 'wts')
        wtsv = self.cnt['wts']
        self.w('scalar', 'wts', wtsv)
        ei = None
        if a:
            ei = a.dma_start(t.woT_h, t.wqT[:])
            ei.then_inc(self.sems['wth'], 16)
        wth = self.inc('wth', 16)
        self.w('tensor', 'wth', wth)
        self.trans_1024(t.wqb, t.wqT, 'wtt', 'wts')
        wtb = self.fin(te.nop() if te else None, 'wtb')

        for f in range(NFD):
            self.w('tensor', 'm2v', self.cnt['m2v'])
            last = None
            if te:
                last = te.transpose(t.pitp[0][:16, 0:128], t.E[:, f * 16:(f + 1) * 16], t.ident[:])
            m2t = self.fin(last, 'm2t')
            self.w('vector', 'm2t', m2t)
            mi = None
            if v:
                mi = v.tensor_copy(t.M2[:16, f * 128:(f + 1) * 128], t.pitp[0][:16, 0:128])
            self.fin(mi, 'm2v')

    def prep(self):
        t = self.t
        g = self.on('gpsimd')
        self.w('gpsimd', 'dsc', self.cnt['dsc'])

        def pldsem(kt):
            return 'pldA' if kt % 2 == 0 else 'pldB'

        def k2load(kt):
            if g:
                g.dma_start(t.pA2[kt % 2][:, 0:1024],
                            t.k2.rearrange("(a p) f -> p a f", p=128)[:, kt, :]) \
                    .then_inc(self.sems[pldsem(kt)], 16)
            self.inc(pldsem(kt), 16)

        k2load(0)
        k2load(1)
        for kc in range(F2T):
            pc = kc % 2
            wsem = 'pw2A' if pc == 0 else 'pw2B'
            self.w('gpsimd', pldsem(kc), self.cnt[pldsem(kc)])
            self.w('gpsimd', wsem, self.cnt[wsem])
            if g:
                g.tensor_scalar(t.pD2[pc][:], t.pA2[pc][:, 0:1024], 2.0, -1.0,
                                ALU.mult, ALU.add)
                g.dma_start(t.w2s_h[:, kc * 1024:(kc + 1) * 1024], t.pD2[pc][:]) \
                    .then_inc(self.sems[wsem], 16)
            self.inc(wsem, 16)
            if kc + 2 < F2T:
                k2load(kc + 2)


    def a1(self):
        t = self.t
        self.w('scalar', 'ini', 1)
        for c in range(NCH):
            cp = c % 2
            s, v, a, te = self.on('sync'), self.on('vector'), self.on('scalar'), self.on('tensor')
            sl = slice(cp * NT, cp * NT + NT)
            for tt in range(NT):
                gidx = c * NT + tt
                self.w('sync', 'lnf', (c - 1) * NT + tt + 1)
                xsem = f'xld{tt}'
                if s:
                    s.dma_start(t.xbuf4[tt][:], t.x[gidx * 128:(gidx + 1) * 128, :]) \
                        .then_inc(self.sems[xsem], 16)
                xv = self.inc(xsem, 16)
                self.w('vector', xsem, xv)
                if v:
                    lastred = v.tensor_reduce(t.lnsum[:, cp * NT + tt:cp * NT + tt + 1],
                                              t.xbuf4[tt][:], AX.X, ALU.add)
                else:
                    lastred = None
                self.w('scalar', xsem, xv)
                self.w('scalar', 'lnA', c - 1)
                sqi = None
                if a:
                    sqi = a.activation(t.junk[:], t.xbuf4[tt][:], AF.Square,
                                       accum_out=t.ssqs[:, cp * NT + tt:cp * NT + tt + 1])
                self.fin(sqi, 'sqd')
            self.vbar(lastred)
            self.w('vector', 'sqd', self.cnt['sqd'])
            ri = None
            if v:
                v.tensor_scalar_mul(t.lnm[:, sl], t.lnsum[:, sl], 1.0 / D)
                ri = v.tensor_scalar_mul(t.lnt1[:, sl], t.ssqs[:, sl], 1.0 / D)
            self.vbar(ri)
            if v:
                ri = v.tensor_tensor(t.lnmsq[:, sl], t.lnm[:, sl], t.lnm[:, sl], ALU.mult)
            self.vbar(ri)
            lnA = None
            if v:
                lnA = v.tensor_tensor(t.lnvar[:, sl], t.lnt1[:, sl], t.lnmsq[:, sl], ALU.subtract)
            va = self.fin(lnA, 'lnA')
            self.w('scalar', 'lnA', va)
            sdi = None
            if a:
                sdi = a.activation(t.lnsd[:, sl], t.lnvar[:, sl], AF.Sqrt, bias=t.epsv[:])
            vb = self.fin(sdi, 'lnB')
            self.w('vector', 'lnB', vb)
            ri = None
            if v:
                ri = v.reciprocal(t.lnrstd[:, sl], t.lnsd[:, sl])
            rr = self.fin(ri, 'lnr')
            self.w('vector', 'lnr', rr)
            for tt in range(NT):
                gidx = c * NT + tt
                par = gidx % 2
                self.w('vector', 'trp', gidx - 1)
                xni = None
                if v:
                    xni = v.tensor_scalar(t.xnbuf[:, par * D:(par + 1) * D], t.xbuf4[tt][:],
                                          t.lnm[:, cp * NT + tt:cp * NT + tt + 1],
                                          t.lnrstd[:, cp * NT + tt:cp * NT + tt + 1],
                                          ALU.subtract, ALU.mult)
                lnf = self.fin(xni, 'lnf')

                self.w('tensor', 'lnf', lnf)
                self.w('tensor', 'xev', gidx - 1)
                if gidx == 0:
                    self.w('tensor', 'wts', 16)
                last = None
                if te:
                    for f in range(NFD):
                        last = te.transpose(t.ptr[par][:, f * 128:(f + 1) * 128],
                                            t.xnbuf[:, par * D + f * 128: par * D + f * 128 + 128],
                                            t.ident[:])
                trp = self.fin(last, 'trp')

                h = tt // 2
                hidx = 2 * c + h
                slot = hidx % 2
                self.w('scalar', 'trp', trp)
                if tt % 2 == 0 and hidx >= 2:
                    self.w('scalar', 'qkm', NFD * (hidx - 1))
                    self.w('scalar', 'xns', 16 * (hidx - 1))
                ev = None
                if a:
                    outap = t.xnTcH[slot][:].rearrange("p (f tok) -> p f tok", f=NFD)[:, :, (tt % 2) * 128:(tt % 2) * 128 + 128]
                    ev = a.copy(outap, t.ptr[par][:].rearrange("p (f i) -> p f i", f=NFD))
                self.fin(ev, 'xev')
                if tt % 2 == 0:
                    continue
                self.w('scalar', 'xev', self.cnt['xev'])
                if a:
                    dst = t.xnt_h[:, :, c * CH + h * 256: c * CH + (h + 1) * 256].rearrange("f p tok -> p f tok")
                    a.dma_start(dst, t.xnTcH[slot][:].rearrange("p (f tok) -> p f tok", f=NFD)) \
                        .then_inc(self.sems['xns'], 16)
                self.inc('xns', 16)
                self.w('tensor', 'xev', self.cnt['xev'])
                if hidx == 0:
                    self.w('tensor', 'wtb', 1)
                    self.w('tensor', 'm2v', NFD)
                for i in range(NFD):
                    ghidx = hidx * NFD + i
                    bank = t.qkvp[ghidx % 6]
                    self.w('tensor', 'wve', ghidx - 5)
                    last = None
                    if te:
                        for j in range(NFD):
                            last = te.matmul(bank[:, 0:256],
                                             t.wqT[:, j * 1024 + i * 128: j * 1024 + i * 128 + 128],
                                             t.xnTcH[slot][:, j * 256:(j + 1) * 256],
                                             start=(j == 0), stop=(j == NFD - 1))
                    qk = self.fin(last, 'qkm')
                    self.w('scalar', 'qkm', qk)
                    wv = None
                    if a:
                        wv = a.copy(t.wT[:, i * NTOK + c * CH + h * 256: i * NTOK + c * CH + h * 256 + 256],
                                    bank[:, 0:256])
                    self.fin(wv, 'wve')
                    if h == 1:
                        nsq = None
                        if a:
                            nsq = a.activation(t.junkb[:], t.wT[:, i * NTOK + c * CH: i * NTOK + (c + 1) * CH],
                                               AF.Square, accum_out=t.nsqacc[:, i * NCH + c: i * NCH + c + 1])
                        self.fin(nsq, 'nsq')

        v = self.on('vector')
        self.w('vector', 'nsq', NFD * NCH)
        ri = None
        if v:
            ri = v.tensor_reduce(t.rn2a[:], t.nsqacc[:].rearrange("p (f c) -> p f c", f=NFD), AX.X, ALU.add)
        self.vbar(ri)
        if v:
            ri = v.tensor_scalar_max(t.rn2b[:], t.rn2a[:], 1e-24)
        self.vbar(ri)
        if v:
            ri = v.reciprocal(t.rn2[:], t.rn2b[:])
        rn = self.fin(ri, 'rnd')
        self.w('vector', 'rnd', rn)
        ri = None
        if v:
            for f in range(NFD):
                ri = v.tensor_scalar_mul(t.E[:, f * 16:(f + 1) * 16], t.E[:, f * 16:(f + 1) * 16],
                                         t.rn2[:, f:f + 1])
        self.vbar(ri)
        ni = self.on('vector').nop() if self.on('vector') else None
        self.fin(ni, 'epv')

    def a2_front1a(self, c):
        t = self.t
        cp = c % 2
        v, te = self.on('vector'), self.on('tensor')
        a = self.on('scalar')
        self.w('scalar', 'emm', c - 1)
        self.w('scalar', 'ust', NFD * (c - 1))
        if c == 0:
            self.w('scalar', 'qkm', 2 * NFD * NCH)
            self.w('scalar', 'xns', 32 * NCH)
        wni = None
        for f in range(NFD):
            if a:
                wni = a.activation(t.wn2f[cp][:, f * CH:(f + 1) * CH],
                                   t.wT[:, f * NTOK + c * CH: f * NTOK + (c + 1) * CH],
                                   AF.Square)
        wn = self.fin(wni, 'wnv')
        self.w('tensor', 'wnv', wn)
        if c == 0:
            self.w('tensor', 'm2v', NFD)
            self.w('tensor', 'epv', 1)
        self.w('tensor', 'lgv', c - 1)
        last = None
        if te:
            for f in range(NFD):
                last = te.matmul(t.lgp[cp][:16, :], t.E[:, f * 16:(f + 1) * 16],
                                 t.wn2f[cp][:, f * CH:(f + 1) * CH],
                                 start=(f == 0), stop=(f == NFD - 1))
        self.fin(last, 'emm')

    def a2_front1b(self, c):
        t = self.t
        cp = c % 2
        v, te = self.on('vector'), self.on('tensor')
        self.w('vector', 'emm', c + 1)
        self.w('vector', 'lgt', c - 1)
        lv = None
        if v:
            lv = v.tensor_copy(t.lgT[:16, cp * CH:(cp + 1) * CH], t.lgp[cp][:16, :])
        lgv = self.fin(lv, 'lgv')
        self.w('tensor', 'lgv', lgv)
        self.w('tensor', 'pnv', c - 1)
        last = None
        if te:
            for q in range(NT):
                last = te.transpose(t.lgnp[cp][:, q * 16:(q + 1) * 16],
                                    t.lgT[:16, cp * CH + q * 128: cp * CH + (q + 1) * 128],
                                    t.ident[:16, 0:16])
        self.fin(last, 'lgt')

    def a2_front2(self, c):
        t = self.t
        cp = c % 2
        v, te, a = self.on('vector'), self.on('tensor'), self.on('scalar')
        self.w('vector', 'lgt', c + 1)
        lt3 = t.ltbuf[:, cp * 64:(cp + 1) * 64].rearrange("p (q h) -> p q h", q=NT)
        lti = None
        if v:
            ln3 = t.lgnp[cp][:].rearrange("p (q h) -> p q h", q=NT)
            tb3 = t.tempB[:].rearrange("p (o h) -> p o h", o=1)
            ln3b, tb3b = bcast(ln3, tb3)
            lti = v.tensor_tensor(lt3, ln3b, tb3b, ALU.mult)
        self.fin(lti, 'pnv')
        sfi = None
        if v:
            sfi = v.tensor_reduce(t.nmx4[:, cp * 4:(cp + 1) * 4], lt3, AX.X, ALU.max, negate=True)
        self.vbar(sfi)
        if v:
            nm3 = t.nmx4[:, cp * 4:(cp + 1) * 4].rearrange("p (q o) -> p q o", o=1)
            lt3b, nm3b = bcast(lt3, nm3)
            sfi = v.tensor_tensor(t.ebuf[:, cp * 64:(cp + 1) * 64].rearrange("p (q h) -> p q h", q=NT),
                                  lt3b, nm3b, ALU.add)
        sfa = self.fin(sfi, 'sfa')
        self.w('scalar', 'sfa', sfa)
        self.w('scalar', 'pnd', c - 1)
        ei = None
        if a:
            ei = a.activation(t.ebuf2[:, cp * 64:(cp + 1) * 64],
                              t.ebuf[:, cp * 64:(cp + 1) * 64], AF.Exp)
        sfb = self.fin(ei, 'sfb')
        self.w('vector', 'sfb', sfb)
        self.w('vector', 'ptm', c - 1)
        pni = None
        e3 = None
        if v:
            e3 = t.ebuf2[:, cp * 64:(cp + 1) * 64].rearrange("p (q h) -> p q h", q=NT)
            pni = v.tensor_reduce(t.s4[:, cp * 4:(cp + 1) * 4], e3, AX.X, ALU.add)
        self.vbar(pni)
        if v:
            pni = v.reciprocal(t.rs4[:, cp * 4:(cp + 1) * 4], t.s4[:, cp * 4:(cp + 1) * 4])
        self.vbar(pni)
        if v:
            rs3 = t.rs4[:, cp * 4:(cp + 1) * 4].rearrange("p (q o) -> p q o", o=1)
            e3b, rs3b = bcast(e3, rs3)
            pni = v.tensor_tensor(t.pinb[:, cp * 64:(cp + 1) * 64].rearrange("p (q h) -> p q h", q=NT),
                                  e3b, rs3b, ALU.mult)
        self.fin(pni, 'pnd')
        self.w('tensor', 'pnd', c + 1)
        self.w('tensor', 'ptv', c - 1)
        last = None
        if te:
            for q in range(NT):
                last = te.transpose(t.pitp[cp][:16, q * 128:(q + 1) * 128],
                                    t.pinb[:, cp * 64 + q * 16: cp * 64 + (q + 1) * 16],
                                    t.ident[:])
        self.fin(last, 'ptm')

    def a2_back(self, c):
        t = self.t
        cp = c % 2
        v, te, s = self.on('vector'), self.on('tensor'), self.on('sync')
        self.w('vector', 'ptm', c + 1)
        self.w('vector', 'pbm', NFD * (c - 1))
        self.w('vector', 'ptsA' if cp == 0 else 'ptsB', 16 * (c // 2))
        pv = None
        if v:
            v.tensor_copy(t.piTc[:16, cp * CH:(cp + 1) * CH], t.pitp[cp][:16, :])
            pv = v.tensor_reduce(t.Sacc[:16, c:c + 1],
                                 t.piTc[:16, cp * CH:(cp + 1) * CH], AX.X, ALU.add)
        ptv = self.fin(pv, 'ptv')
        self.w('sync', 'ptv', ptv)
        if s:
            s.dma_start(t.pit_h[:, c * CH:(c + 1) * CH],
                        t.piTc[:16, cp * CH:(cp + 1) * CH]) \
                .then_inc(self.sems['ptsA' if cp == 0 else 'ptsB'], 16)
        self.inc('ptsA' if cp == 0 else 'ptsB', 16)

        for f in range(NFD):
            idx = c * NFD + f
            bank = t.pibu[idx % 2]
            self.w('tensor', 'ptv', ptv)
            self.w('tensor', 'ust', idx - 1)
            last = None
            if te:
                last = te.matmul(bank[:], t.M2[:16, f * 128:(f + 1) * 128],
                                 t.piTc[:16, cp * CH:(cp + 1) * CH],
                                 start=True, stop=True)
            pb = self.fin(last, 'pbm')
            self.w('vector', 'pbm', pb)
            ui = None
            if v:
                ui = v.scalar_tensor_tensor(t.ujunk[:], t.wn2f[cp][:, f * CH:(f + 1) * CH],
                                            0.0, bank[:], ALU.bypass, ALU.mult,
                                            accum_out=t.Uacc[:, f * NCH + c: f * NCH + c + 1])
            self.fin(ui, 'ust')

    def a2(self):
        t = self.t
        for c in range(NCH):
            self.a2_front1a(c)
            if c >= 1:
                self.a2_back(c - 1)
            self.a2_front1b(c)
            self.a2_front2(c)
        self.a2_back(NCH - 1)

        v, g = self.on('vector'), self.on('gpsimd')
        self.w('vector', 'ust', NFD * NCH)
        sv = None
        if v:
            sv = v.tensor_reduce(t.Sfin[:16, :], t.Sacc[:16, :], AX.X, ALU.add)
        ssv = self.fin(sv, 'ssv')
        self.w('vector', 'ssv', ssv)
        ni = None
        if v:
            ni = v.nop()
        ssw = self.fin(ni, 'ssw')
        self.w('gpsimd', 'ssw', ssw)
        if g:
            g.dma_start(t.s_h, t.Sfin[:16, :]).then_inc(self.sems['dsc'], 16)
        d3 = self.inc('dsc', 16)
        self.w('gpsimd', 'dsc', d3)
        if g:
            g.dma_start(t.SBfull[:], bass.AP(t.s_h.tensor, 0, [[0, 128], [1, 16]])) \
                .then_inc(self.sems['dsc'], 16)
        d4 = self.inc('dsc', 16)
        self.w('vector', 'dsc', d4)
        ci = None
        if v:
            for f in range(NFD):
                v.tensor_copy(t.SB[0:64, f:f + 1], t.SBfull[0:64, 2 * f:2 * f + 1])
                ci = v.tensor_copy(t.SB[64:128, f:f + 1], t.SBfull[64:128, 2 * f + 1:2 * f + 2])
        self.vbar(ci)
        if v:
            ci = v.tensor_reduce(t.Ufin[:], t.Uacc[:].rearrange("p (f c) -> p f c", f=NFD), AX.X, ALU.add)
        self.vbar(ci)
        if v:
            ci = v.tensor_scalar_add(t.ct1[:], t.SB[:], 1e-8)
        self.vbar(ci)
        if v:
            ci = v.reciprocal(t.ct2[:], t.ct1[:])
        self.vbar(ci)
        if v:
            ci = v.tensor_tensor(t.ct3[:], t.Ufin[:], t.ct2[:], ALU.mult)
        self.vbar(ci)
        if v:
            ci = v.tensor_scalar_add(t.ct4[:], t.ct3[:], 1.0)
        self.vbar(ci)
        if v:
            ci = v.reciprocal(t.ct5[:], t.ct4[:])
        self.vbar(ci)
        if v:
            ci = v.tensor_scalar_mul(t.Cneg[:], t.ct5[:], -1.0)
        cd = self.fin(ci, 'cdn')
        self.w('vector', 'cdn', cd)

    def a3(self):
        t = self.t
        s = self.on('sync')
        self.w('sync', 'qkm', 2 * NFD * NCH)
        self.w('sync', 'ust', NFD * NCH)
        if s:
            s.dma_start(t.wqT[:], t.woT_h).then_inc(self.sems['wol'], 16)
        wol = self.inc('wol', 16)

        for c in range(NCH):
            s = self.on('sync')
            self.w('sync', 'pb3', NFD * (c - 1))
            if c < 2:
                self.w('sync', 'pbm', NFD * NCH)
            self.w('sync', 'ptsA' if c % 2 == 0 else 'ptsB', 16 * (c // 2 + 1))
            ptlsem = 'ptlA' if c % 2 == 0 else 'ptlB'
            if s:
                s.dma_start(t.piTc[:16, (c % 2) * CH:(c % 2 + 1) * CH],
                            t.pit_h[:, c * CH:(c + 1) * CH]).then_inc(self.sems[ptlsem], 16)
            ptl = self.inc(ptlsem, 16)

            te, v = self.on('tensor'), self.on('vector')
            for f in range(NFD):
                idx = c * NFD + f
                bank = t.pib[idx % 2]
                self.w('tensor', ptlsem, ptl)
                if idx == 0:
                    self.w('tensor', 'ust', NFD * NCH)
                self.w('tensor', 'osd', idx - 1)
                last = None
                if te:
                    last = te.matmul(bank[:], t.M2[:16, f * 128:(f + 1) * 128],
                                     t.piTc[:16, (c % 2) * CH:(c % 2 + 1) * CH],
                                     start=True, stop=True)
                pb3 = self.fin(last, 'pb3')
                self.w('vector', 'pb3', pb3)
                if c > 0 and f == 0:
                    self.w('vector', 'aom', 2 * NT * c)
                oi = None
                if v:
                    oi = v.scalar_tensor_tensor(t.oTbuf[:, f * CH:(f + 1) * CH],
                                                t.wT[:, f * NTOK + c * CH: f * NTOK + (c + 1) * CH],
                                                t.Cneg[:, f:f + 1], bank[:],
                                                ALU.mult, ALU.mult)
                self.fin(oi, 'osd')

            s = self.on('sync')
            if c == 0:
                self.w('sync', 'lnf', NTILES)
            for tt in range(NT):
                gidx = c * NT + tt
                self.w('sync', 'rev', 2 * (gidx - 1))
                if s:
                    s.dma_start(t.xbuf[tt % 2][:], t.x[gidx * 128:(gidx + 1) * 128, :]) \
                        .then_inc(self.sems['xl2A' if gidx % 2 == 0 else 'xl2B'], 16)
                self.inc('xl2A' if gidx % 2 == 0 else 'xl2B', 16)

            te = self.on('tensor')
            self.w('tensor', 'osd', self.cnt['osd'])
            if c == 0:
                self.w('tensor', 'wol', wol)
            for tt in range(NT):
                for nh in range(2):
                    aidx = (c * NT + tt) * 2 + nh
                    bank = t.aop[aidx % 3]
                    self.w('tensor', 'rev', aidx - 2)
                    last = None
                    if te:
                        for f in range(NFD):
                            last = te.matmul(bank[:],
                                             t.oTbuf[:, f * CH + tt * 128: f * CH + tt * 128 + 128],
                                             t.wqT[:, f * 1024 + nh * CH: f * 1024 + (nh + 1) * CH],
                                             start=(f == 0), stop=False)
                        last = te.matmul(bank[:], t.onesrow[:], t.b_bf[0:1, nh * CH:(nh + 1) * CH],
                                         start=False, stop=True)
                    self.fin(last, 'aom')

            v, a = self.on('vector'), self.on('scalar')
            for tt in range(NT):
                gidx = c * NT + tt
                self.w('vector', 'aom', (c * NT + tt) * 2 + 2)
                self.w('vector', 'xl2A' if gidx % 2 == 0 else 'xl2B', 16 * (gidx // 2 + 1))
                self.w('vector', 'rstA' if gidx % 2 == 0 else 'rstB', 16 * (gidx // 2))
                for nh in range(2):
                    aidx = (c * NT + tt) * 2 + nh
                    ri = None
                    if v:
                        ri = v.tensor_tensor(t.Rbuf[:, (tt % 2) * D + nh * CH:(tt % 2) * D + (nh + 1) * CH],
                                             t.aop[aidx % 3][:],
                                             t.xbuf[tt % 2][:, nh * CH:(nh + 1) * CH], ALU.add)
                    self.fin(ri, 'rev')
                rv = self.cnt['rev']
                self.w('scalar', 'rev', rv)
                if a:
                    a.dma_start(t.r_h[gidx * 128:(gidx + 1) * 128, :],
                                t.Rbuf[:, (tt % 2) * D:(tt % 2 + 1) * D]) \
                        .then_inc(self.sems['rstA' if gidx % 2 == 0 else 'rstB'], 16)
                self.inc('rstA' if gidx % 2 == 0 else 'rstB', 16)

    def f(self):
        t = self.t
        s, te, a, v = self.on('sync'), self.on('tensor'), self.on('scalar'), self.on('vector')

        self.w('sync', 'osd', NFD * NCH)
        self.w('sync', 'pb3', NFD * NCH)
        self.w('sync', 'rstA', 16 * (NTILES // 2))
        self.w('sync', 'rstB', 16 * (NTILES // 2))
        for pw in ['pw2A', 'pw2B', 'pw1A', 'pw1B', 'pw0A', 'pw0B']:
            self.w('sync', pw, self.cnt[pw])
        if s:
            s.dma_start(t.w2res[:], t.w2s_h).then_inc(self.sems['w2l'], 16)
        w2l = self.inc('w2l', 16)

        for c in range(NCH):
            self.w('sync', 'h1m', F1T * c)
            if s:
                src = t.xnt_h[:, :, c * CH:(c + 1) * CH].rearrange("f p tok -> p f tok")
                if c == 0:
                    s.wait_ge(self.sems['xns'], 32 * NCH)
                s.dma_start(t.xnl[:].rearrange("p (f tok) -> p f tok", f=NFD), src) \
                    .then_inc(self.sems['xnl'], 16)
            xnl = self.inc('xnl', 16)

            for fi in range(FPC):
                fg = c * FPC + fi
                par = fg % 2
                par3 = fg % 3
                fsem = ['wflA', 'wflB', 'wflC'][par3]
                ldq = 'sync' if fi < 8 else 'gpsimd'
                qh = self.on(ldq)
                self.w(ldq, 'wcv', fg - 2)
                if fi < 2:
                    self.w(ldq, 'wcv', (c - 1) * FPC + fi + 7)
                elif fi < 8:
                    self.w(ldq, 'wcv', fg - 1)
                if qh:
                    if fi < 8:
                        qh.dma_start(t.u8p[par3][:], t.k0ts[fi]).then_inc(self.sems[fsem], 16)
                        qh.dma_start(t.u81[par][:], t.k0bs[fi]).then_inc(self.sems[fsem], 16)
                    else:
                        qh.dma_start(t.u8p[par3][:], t.k1s[fi - 8]).then_inc(self.sems[fsem], 16)
                self.inc(fsem, 32 if fi < 8 else 16)
                vd = self.on('vector')
                self.w('vector', fsem, self.cnt[fsem])
                if fg >= 2:
                    cp2, fip = (fg - 2) // FPC, (fg - 2) % FPC
                    if fip < 8:
                        self.w('vector', 'h1m', cp2 * F1T + 4 * (fip + 1))
                    else:
                        self.w('vector', 'h2m', cp2 * F2T + (fip - 8) + 1)
                ci = None
                if vd:
                    if fi < 8:
                        ci = vd.tensor_tensor(t.wbuf[par][:], t.u8p[par3][:], t.u81[par][:],
                                              ALU.subtract)
                    else:
                        ci = vd.tensor_scalar(t.wbuf[par][:], t.u8p[par3][:], 2.0, -1.0,
                                              ALU.mult, ALU.add)
                self.fin(ci, 'wcv')

            self.w('tensor', 'xnl', xnl)
            if c == 0:
                self.w('tensor', 'w2l', w2l)
            for m1 in range(F1T):
                fg = c * FPC + m1 // 4
                self.w('tensor', 'wcv', fg + 1)
                self.w('tensor', 'h1e', c * F1T + m1 - 3)
                bank = t.hb[m1 % 4]
                last = None
                if te:
                    for k in range(NFD):
                        last = te.matmul(bank[:],
                                         t.wbuf[fg % 2][:, (m1 % 4) * 1024 + k * 128:(m1 % 4) * 1024 + k * 128 + 128],
                                         t.xnl[:, k * CH:(k + 1) * CH],
                                         start=(k == 0), stop=(k == NFD - 1))
                h1m = self.fin(last, 'h1m')
                self.w('scalar', 'h1m', h1m)
                if c > 0 and m1 == 0:
                    self.w('scalar', 'h2m', F2T * c)
                ei = None
                if a:
                    ei = a.activation(t.h1t[:, m1 * CH:(m1 + 1) * CH], bank[:], AF.Relu,
                                      scale=t.scalsB[:, 0:1])
                self.fin(ei, 'h1e')

            self.w('tensor', 'h1e', F1T * (c + 1))
            for m2 in range(F2T):
                fg = c * FPC + 8 + m2
                self.w('tensor', 'wcv', fg + 1)
                self.w('tensor', 'h2e', c * F2T + m2 - 3)
                bank = t.hb[m2 % 4]
                last = None
                if te:
                    for k in range(F1T):
                        last = te.matmul(bank[:], t.wbuf[fg % 2][:, k * 128:(k + 1) * 128],
                                         t.h1t[:, k * CH:(k + 1) * CH],
                                         start=(k == 0), stop=(k == F1T - 1))
                h2m = self.fin(last, 'h2m')
                self.w('scalar', 'h2m', h2m)
                if c > 0 and m2 == 0:
                    self.w('scalar', 'ym', 2 * NT * c)
                ei = None
                if a:
                    ei = a.activation(t.h2t[:, m2 * CH:(m2 + 1) * CH], bank[:], AF.Relu,
                                      scale=t.scalsB[:, 1:2])
                self.fin(ei, 'h2e')

            for tt in range(NT):
                gidx = c * NT + tt
                self.w('sync', 'yev', 2 * (gidx - 1))
                if s:
                    s.dma_start(t.xbuf[tt % 2][:], t.r_h[gidx * 128:(gidx + 1) * 128, :]) \
                        .then_inc(self.sems['rldA' if gidx % 2 == 0 else 'rldB'], 16)
                self.inc('rldA' if gidx % 2 == 0 else 'rldB', 16)

            self.w('tensor', 'h2e', F2T * (c + 1))
            for tt in range(NT):
                for nh in range(2):
                    yidx = (c * NT + tt) * 2 + nh
                    bank = t.yb[yidx % 4]
                    self.w('tensor', 'yev', yidx - 3)
                    last = None
                    if te:
                        for k in range(F2T):
                            last = te.matmul(bank[:],
                                             t.h2t[:, k * CH + tt * 128: k * CH + tt * 128 + 128],
                                             t.w2res[:, k * 1024 + nh * CH: k * 1024 + (nh + 1) * CH],
                                             start=(k == 0), stop=(k == F2T - 1))
                    self.fin(last, 'ym')

                gidx = c * NT + tt
                self.w('vector', 'ym', (c * NT + tt) * 2 + 2)
                self.w('vector', 'rldA' if gidx % 2 == 0 else 'rldB', 16 * (gidx // 2 + 1))
                self.w('vector', 'ostA' if gidx % 2 == 0 else 'ostB', 16 * (gidx // 2))
                if c == 0 and tt == 0:
                    self.w('vector', 'rstA', 16 * (NTILES // 2))
                    self.w('vector', 'rstB', 16 * (NTILES // 2))
                for nh in range(2):
                    yidx = (c * NT + tt) * 2 + nh
                    yi = None
                    if v:
                        yi = v.scalar_tensor_tensor(t.Rbuf[:, (tt % 2) * D + nh * CH:(tt % 2) * D + (nh + 1) * CH],
                                                    t.yb[yidx % 4][:], t.scalsB[:, 2:3],
                                                    t.xbuf[tt % 2][:, nh * CH:(nh + 1) * CH],
                                                    ALU.mult, ALU.add)
                    self.fin(yi, 'yev')
                yv = self.cnt['yev']
                self.w('scalar', 'yev', yv)
                if a:
                    a.dma_start(t.out[gidx * 128:(gidx + 1) * 128, :],
                                t.Rbuf[:, (tt % 2) * D:(tt % 2 + 1) * D]) \
                        .then_inc(self.sems['ostA' if gidx % 2 == 0 else 'ostB'], 16)
                self.inc('ostA' if gidx % 2 == 0 else 'ostB', 16)

    def drain_dma_sems(self):
        for eng, sems in [('sync', ['ild', 'xld0', 'xld1', 'xld2', 'xld3', 'xl2A', 'xl2B',
                                    'ptlA', 'ptlB', 'wol', 'w2l',
                                    'wflA', 'wflB', 'wflC', 'xnl', 'rldA', 'rldB', 'ptsA', 'ptsB']),
                          ('scalar', ['wth', 'xns', 'rstA', 'rstB', 'ostA', 'ostB']),
                          ('gpsimd', ['dsc', 'pldA', 'pldB', 'pw2A', 'pw2B',
                                      'pw1A', 'pw1B', 'pw0A', 'pw0B',
                                      'wflA', 'wflB', 'wflC', 'dbg'])]:
            for sm in sems:
                if self.cnt[sm] > 0:
                    self.w(eng, sm, self.cnt[sm])

    def tail(self):
        t = self.t
        g = self.on('gpsimd')
        if self.phases >= 4 and not self.debug:
            self.w('scalar', 'ostA', 16 * (NTILES // 2))
            self.w('scalar', 'ostB', 16 * (NTILES // 2))
            self.drain_dma_sems()
            return
        if self.debug:
            self.w('gpsimd', 'wve', 2 * NFD * NCH)
            self.w('gpsimd', 'nsq', NFD * NCH)
            self.w('gpsimd', 'rnd', 1)
            if self.phases >= 2:
                self.w('gpsimd', 'ptsA', 16 * (NCH // 2))
                self.w('gpsimd', 'ptsB', 16 * (NCH // 2))
                self.w('gpsimd', 'cdn', 1)
            if self.phases >= 3:
                self.w('gpsimd', 'rstA', 16 * (NTILES // 2))
                self.w('gpsimd', 'rstB', 16 * (NTILES // 2))
            if self.phases >= 4:
                self.w('gpsimd', 'ostA', 16 * (NTILES // 2))
                self.w('gpsimd', 'ostB', 16 * (NTILES // 2))
            nd = 0
            if self.phases < 4:
                if g:
                    g.dma_start(t.dwT, t.wT[:, 0:2048]).then_inc(self.sems['dbg'], 16)
                nd += 16
            if g:
                g.dma_start(t.dnsq, t.nsqacc[:]).then_inc(self.sems['dbg'], 16)
                g.dma_start(t.drn2, t.rn2[:]).then_inc(self.sems['dbg'], 16)
                g.dma_start(t.dxn, t.xnt_h[0, :, 0:1024]).then_inc(self.sems['dbg'], 16)
            nd += 3 * 16
            if self.phases >= 2:
                if g:
                    g.dma_start(t.dpit, t.pit_h[:, 0:1024]).then_inc(self.sems['dbg'], 16)
                    g.dma_start(t.dC, t.Cneg[:]).then_inc(self.sems['dbg'], 16)
                nd += 2 * 16
            if self.phases >= 3:
                if g:
                    g.dma_start(t.dR[0:128, :], t.r_h[0:128, :]).then_inc(self.sems['dbg'], 16)
                    g.dma_start(t.dR[128:256, :], t.r_h[4096:4224, :]).then_inc(self.sems['dbg'], 16)
                nd += 2 * 16
            if self.phases >= 4:
                if g:
                    g.dma_start(t.dh1, t.h1t[:]).then_inc(self.sems['dbg'], 16)
                    g.dma_start(t.dh2, t.h2t[:]).then_inc(self.sems['dbg'], 16)
                    g.dma_start(t.dw0, t.w0s_h[0]).then_inc(self.sems['dbg'], 16)
                    g.dma_start(t.dw1, t.w1s_h[0]).then_inc(self.sems['dbg'], 16)
                    g.dma_start(t.dw2, t.w2s_h[:, 0:4096]).then_inc(self.sems['dbg'], 16)
                nd += 5 * 16
            self.inc('dbg', nd)
            self.w('gpsimd', 'dbg', nd)
        else:
            if self.phases >= 3:
                self.w('scalar', 'rstA', 16 * (NTILES // 2))
                self.w('scalar', 'rstB', 16 * (NTILES // 2))
            else:
                self.w('scalar', 'wve', 2 * NFD * NCH)
        self.drain_dma_sems()


class T:
    pass


def build(phases=4, debug=False):
    nc = bass.Bass("TRN2", target_bir_lowering=False, detect_race_conditions=False)
    t = T()

    t.x = nc.dram_tensor("x", [NTOK, D], F32, kind="ExternalInput").ap()
    t.wqkv = nc.dram_tensor("wqkv", [D, D], F32, kind="ExternalInput").ap()
    t.temp = nc.dram_tensor("temp", [16, 1], F32, kind="ExternalInput").ap()
    t.wout = nc.dram_tensor("wout", [D, D], F32, kind="ExternalInput").ap()
    t.bout = nc.dram_tensor("bout", [1, D], F32, kind="ExternalInput").ap()
    t.k1s = nc.dram_tensor("k1s", [32, 128, 4096], U8, kind="ExternalInput").ap()
    t.k0ts = nc.dram_tensor("k0ts", [8, 128, 4096], U8, kind="ExternalInput").ap()
    t.k0bs = nc.dram_tensor("k0bs", [8, 128, 4096], U8, kind="ExternalInput").ap()
    t.k2 = nc.dram_tensor("k2", [4096, 1024], U8, kind="ExternalInput").ap()
    t.s0 = nc.dram_tensor("s0", [1, 1], F32, kind="ExternalInput").ap()
    t.s1 = nc.dram_tensor("s1", [1, 1], F32, kind="ExternalInput").ap()
    t.s2 = nc.dram_tensor("s2", [1, 1], F32, kind="ExternalInput").ap()
    t.out = nc.dram_tensor("out", [NTOK, D], F32, kind="ExternalOutput").ap()

    t.xnt_h = nc.dram_tensor("xnt_h", [NFD, 128, NTOK], BF16, kind="Internal").ap()
    t.r_h = nc.dram_tensor("r_h", [NTOK, D], F32, kind="Internal").ap()
    t.pit_h = nc.dram_tensor("pit_h", [16, NTOK], BF16, kind="Internal").ap()
    t.w2s_h = nc.dram_tensor("w2s_h", [128, F2T * 1024], BF16, kind="Internal").ap()
    t.der_h = nc.dram_tensor("der_h", [1, 8], F32, kind="Internal").ap()
    t.s_h = nc.dram_tensor("s_h", [16, 1], F32, kind="Internal").ap()
    t.woT_h = nc.dram_tensor("woT_h", [128, NFD * 1024], BF16, kind="Internal").ap()

    if debug:
        t.dxn = nc.dram_tensor("dxn", [128, 1024], BF16, kind="ExternalOutput").ap()
        t.dwT = nc.dram_tensor("dwT", [128, 2048], BF16, kind="ExternalOutput").ap()
        t.dnsq = nc.dram_tensor("dnsq", [128, 128], F32, kind="ExternalOutput").ap()
        t.drn2 = nc.dram_tensor("drn2", [128, 8], F32, kind="ExternalOutput").ap()
        if phases >= 2:
            t.dpit = nc.dram_tensor("dpit", [16, 1024], BF16, kind="ExternalOutput").ap()
            t.dC = nc.dram_tensor("dC", [128, 8], F32, kind="ExternalOutput").ap()
        if phases >= 3:
            t.dR = nc.dram_tensor("dR", [256, D], F32, kind="ExternalOutput").ap()
        if phases >= 4:
            t.dh1 = nc.dram_tensor("dh1", [128, F1T * CH], BF16, kind="ExternalOutput").ap()
            t.dh2 = nc.dram_tensor("dh2", [128, F2T * CH], BF16, kind="ExternalOutput").ap()
            t.dw0 = nc.dram_tensor("dw0", [128, 4096], BF16, kind="ExternalOutput").ap()
            t.dw1 = nc.dram_tensor("dw1", [128, 4096], BF16, kind="ExternalOutput").ap()
            t.dw2 = nc.dram_tensor("dw2", [128, 4096], BF16, kind="ExternalOutput").ap()

    sb = nc.alloc_sbuf_tensor
    t.zv = sb("zv", [128, 1], F32)
    t.ov = sb("ov", [128, 1], F32)
    t.epsv = sb("epsv", [128, 1], F32)
    t.onesrow = sb("onesrow", [1, 128], BF16)
    t.ident = sb("ident", [128, 128], BF16)
    t.E = sb("E", [128, 128], BF16)
    t.M2 = sb("M2", [16, NFD * 128], BF16)
    t.tempB = sb("tempB", [128, 16], F32)
    t.scalsB = sb("scalsB", [128, 8], F32)
    t.svec = sb("svec", [1, 4], F32)
    t.der = sb("der", [1, 8], F32)
    t.rn2 = sb("rn2", [128, NFD], F32)
    t.rn2a = sb("rn2a", [128, NFD], F32)
    t.rn2b = sb("rn2b", [128, NFD], F32)
    t.SB = sb("SBt", [128, NFD], F32)
    t.SBfull = sb("SBfull", [128, 16], F32)
    t.Cneg = sb("Cneg", [128, NFD], F32)
    t.nsqacc = sb("nsqacc", [128, 128], F32)
    t.Uacc = sb("Uacc", [128, 128], F32)
    t.Ufin = sb("Ufin", [128, NFD], F32)
    t.Sacc = sb("Sacc", [16, 16], F32)
    t.Sfin = sb("Sfin", [16, 1], F32)
    t.ct1 = sb("ct1", [128, NFD], F32)
    t.ct2 = sb("ct2", [128, NFD], F32)
    t.ct3 = sb("ct3", [128, NFD], F32)
    t.ct4 = sb("ct4", [128, NFD], F32)
    t.ct5 = sb("ct5", [128, NFD], F32)
    t.b_bf = sb("b_bf", [1, D], BF16)
    for nm in ["lnsum", "lnm", "lnmsq", "lnt1", "ssqs", "lnvar", "lnsd", "lnrstd"]:
        setattr(t, nm, sb(nm, [128, 2 * NT], F32))
    t.ltbuf = sb("ltbuf", [128, 128], F32)
    t.nmx4 = sb("nmx4", [128, 8], F32)
    t.ebuf = sb("ebuf", [128, 128], F32)
    t.ebuf2 = sb("ebuf2", [128, 128], F32)
    t.s4 = sb("s4", [128, 8], F32)
    t.rs4 = sb("rs4", [128, 8], F32)
    t.pinb = sb("pinb", [128, 128], BF16)
    t.lgT = sb("lgT", [16, 2 * CH], BF16)
    t.Ufin2 = sb("Ufin2", [128, NFD], F32)
    t.junk = sb("junk", [128, D], BF16)
    t.junkb = sb("junkb", [128, CH], BF16)
    t.wn2 = sb("wn2", [128, 2 * CH], BF16)
    xr = sb("xr", [128, 16384], U8)
    xrbase = nc.lookup_mloc(xr).addr
    t.xbuf4 = [nc.alloc_sbuf_tensor_at(f"xb4_{i}", [128, D], F32, offset=xrbase + 4096 * i)
               for i in range(4)]
    t.Rbuf = nc.alloc_sbuf_tensor_at("Rbuf", [128, 2 * D], F32, offset=xrbase + 8192)
    t.xbuf = t.xbuf4[:2]
    t.piTc = sb("piTc", [16, 2 * CH], BF16)

    ARENA = 178176
    arena = nc.alloc_sbuf_tensor("arena", [128, ARENA], U8)
    abase = nc.lookup_mloc(arena).addr

    def at(name, shape, dtype, off):
        return nc.alloc_sbuf_tensor_at(name, shape, dtype, offset=abase + off)

    t.wT = at("wT", [128, NFD * NTOK], BF16, 0)
    t.wqT = at("wqT", [128, NFD * 1024], BF16, 131072)
    t.xnTcH = [at("xnTcH0", [128, NFD * 256], BF16, 147456),
               at("xnTcH1", [128, NFD * 256], BF16, 151552)]
    t.oTbuf = at("oTbuf", [128, NFD * CH], BF16, 147456)
    t.xnbuf = at("xnbuf", [128, 2 * D], BF16, 155648)
    t.wq32 = at("wq32", [128, NFD * 1024], F32, 0)
    t.wo32 = at("wo32", [128, NFD * 1024], F32, 32768)
    t.wqb = at("wqb", [128, NFD * 1024], BF16, 65536)
    t.wob = at("wob", [128, NFD * 1024], BF16, 81920)
    t.b32 = at("b32", [1, D], F32, 98304)
    t.ujunk = at("ujunk", [128, CH], BF16, 139264)
    t.wn2f = [at("wn2fA", [128, NFD * CH], BF16, 147456),
              at("wn2fB", [128, NFD * CH], BF16, 131072)]
    t.pA2 = [at("pA2a", [128, 4096], U8, 159744), at("pA2b", [128, 4096], U8, 163840)]
    t.pD2 = [at("pD2a", [128, 1024], BF16, 167936), at("pD2b", [128, 1024], BF16, 169984)]
    t.u8p = [at("u80a", [128, 4096], U8, 159744), at("u80b", [128, 4096], U8, 163840),
             at("u80c", [128, 4096], U8, 155648)]
    t.u81 = [at("u81a", [128, 4096], U8, 167936), at("u81b", [128, 4096], U8, 172032)]
    t.w2res = at("w2res", [128, F2T * 1024], BF16, 0)
    t.wbuf = [at("wbufA", [128, 4096], BF16, 65536), at("wbufB", [128, 4096], BF16, 73728)]
    t.h1t = at("h1t", [128, F1T * CH], BF16, 81920)
    t.h2t = at("h2t", [128, F2T * CH], BF16, 114688)
    t.xnl = at("xnl", [128, NFD * CH], BF16, 147456)

    psnap = nc.psum_base
    t.ptr = [nc.alloc_psum_tensor("ptr0", [128, 1024], BF16),
             nc.alloc_psum_tensor("ptr1", [128, 1024], BF16)]
    t.qkvp = [nc.alloc_psum_tensor(f"qkvp{i}", [128, CH], F32) for i in range(6)]
    nc.psum_base = psnap
    t.lgp = [nc.alloc_psum_tensor(f"lgp{i}", [16, CH], F32) for i in range(2)]
    t.lgnp = [nc.alloc_psum_tensor(f"lgnp{i}", [128, 64], BF16) for i in range(2)]
    t.pitp = [nc.alloc_psum_tensor(f"pitp{i}", [16, CH], BF16) for i in range(2)]
    t.pibu = [nc.alloc_psum_tensor(f"pibu{i}", [128, CH], F32) for i in range(2)]
    nc.psum_base = psnap
    t.pib = [nc.alloc_psum_tensor(f"pib{i}", [128, CH], F32) for i in range(2)]
    t.aop = [nc.alloc_psum_tensor(f"aop{i}", [128, CH], F32) for i in range(3)]
    nc.psum_base = psnap
    t.hb = [nc.alloc_psum_tensor(f"hb{i}", [128, CH], F32) for i in range(4)]
    t.yb = [nc.alloc_psum_tensor(f"yb{i}", [128, CH], F32) for i in range(4)]

    nc.const_aps.register(F32, 0.0, t.zv[:])
    nc.const_aps.register(F32, 1.0, t.ov[:])

    semnames = ['vbr', 'ini', 'inv', 'ild', 'dsc', 'wtt', 'wts', 'wth', 'wtb', 'm2t', 'm2v',
                'pldA', 'pldB', 'pw2A', 'pw2B', 'pw1A', 'pw1B', 'pw0A', 'pw0B',
                'xld0', 'xld1', 'xld2', 'xld3', 'sqd', 'lnA', 'lnB', 'lnr', 'lnf', 'trp', 'xev', 'xns',
                'qkm', 'wve', 'nsq', 'rnd', 'rns', 'epv',
                'wnv', 'emm', 'lgm', 'lgv', 'lgt', 'pnv', 'sfa', 'sfb', 'pnd',
                'ptm', 'ptv', 'ptsA', 'ptsB', 'pbm', 'ust', 'ssv', 'ssw', 'cdn',
                'wol', 'ptlA', 'ptlB', 'pb3', 'osd', 'aom', 'xl2A', 'xl2B', 'rev', 'rstA', 'rstB',
                'w2l', 'wflA', 'wflB', 'wflC', 'wcv', 'xnl', 'h1m', 'h1e', 'h2m', 'h2e', 'ym', 'yev',
                'rldA', 'rldB', 'ostA', 'ostB', 'dbg']
    stk = ExitStack()
    sems = {n: stk.enter_context(nc.semaphore(n)) for n in semnames}

    builder = B(nc, sems, t, phases, debug)
    with nc.Block() as block:
        @block.sync
        def _(sync):
            builder.run('sync', sync)

        @block.tensor
        def _(tensor):
            builder.run('tensor', tensor)

        @block.vector
        def _(vector):
            builder.run('vector', vector)

        @block.scalar
        def _(scalar):
            builder.run('scalar', scalar)

        @block.gpsimd
        def _(gpsimd):
            builder.run('gpsimd', gpsimd)

    stk.close()
    return nc


def kernel(L0_attn_x, L0_attn_W_qkv, L0_attn_temp, L0_attn_W_out, L0_attn_b_out,
           L0_ffn_k0, L0_ffn_k1, L0_ffn_k2, L0_ffn_s0, L0_ffn_s1, L0_ffn_s2):
    x = np.ascontiguousarray(np.asarray(L0_attn_x, dtype=np.float32))
    shared = {
        "wqkv": np.ascontiguousarray(np.asarray(L0_attn_W_qkv, dtype=np.float32)),
        "temp": np.ascontiguousarray(np.asarray(L0_attn_temp, dtype=np.float32).reshape(16, 1)),
        "wout": np.ascontiguousarray(np.asarray(L0_attn_W_out, dtype=np.float32)),
        "bout": np.ascontiguousarray(np.asarray(L0_attn_b_out, dtype=np.float32).reshape(1, D)),
        "k1s": np.ascontiguousarray(
            np.asarray(L0_ffn_k1).astype(np.uint8).reshape(32, 128, 32, 128)
            .transpose(2, 1, 0, 3).reshape(32, 128, 4096)),
        "k0ts": np.ascontiguousarray(
            np.asarray(L0_ffn_k0).astype(np.uint8)[:1024].reshape(8, 128, 8, 4, 128)
            .transpose(2, 1, 3, 0, 4).reshape(8, 128, 4096)),
        "k0bs": np.ascontiguousarray(
            np.asarray(L0_ffn_k0).astype(np.uint8)[1024:].reshape(8, 128, 8, 4, 128)
            .transpose(2, 1, 3, 0, 4).reshape(8, 128, 4096)),
        "k2": np.ascontiguousarray(np.asarray(L0_ffn_k2).astype(np.uint8)),
        "s0": np.ascontiguousarray(np.asarray(L0_ffn_s0, dtype=np.float32).reshape(1, 1)),
        "s1": np.ascontiguousarray(np.asarray(L0_ffn_s1, dtype=np.float32).reshape(1, 1)),
        "s2": np.ascontiguousarray(np.asarray(L0_ffn_s2, dtype=np.float32).reshape(1, 1)),
    }
    nc = build()
    in_maps = [dict(shared, x=np.ascontiguousarray(x[b])) for b in range(8)]
    res = run_bass_kernel_spmd(nc, in_maps, core_ids=list(range(8)))
    return np.stack([r["out"] for r in res.results], axis=0)
